# version 03
# baseline (speedup 1.0000x reference)
import numpy as np
import ml_dtypes
import concourse.bass as bass
import concourse.mybir as mybir
from concourse.bass_utils import run_bass_kernel_spmd

F32 = mybir.dt.float32
BF16 = mybir.dt.bfloat16
AF = mybir.ActivationFunctionType
ALU = mybir.AluOpType
AXX = mybir.AxisListType.X

D = 2048
T = 2048
ND = 16
PAN = 1024
ACH = 256
NACH = T // ACH
INTER = 1408
NEXP = 9
EPS_LN = 1e-5
EPS_RMS = float(np.finfo(np.float32).eps)

ENGINES = ("sync", "vector", "scalar", "tensor", "gpsimd")
_uid = [0]


def uname(p):
    _uid[0] += 1
    return f"{p}{_uid[0]}"


class Chan:
    __slots__ = ("sem", "n")

    def __init__(self, sem):
        self.sem = sem
        self.n = 0


class KB:
    def __init__(self):
        self.nc = bass.Bass("TRN2", target_bir_lowering=False)
        self.ops = {e: [] for e in ENGINES}
        self.chans = []
        self.bankfree = {}
        self.pending = []

    def chan(self, name):
        c = Chan(self.nc.alloc_semaphore(name=uname(name)))
        self.chans.append(c)
        return c

    def emit(self, eng, fn, chan=None, inc=1):
        if chan is not None:
            chan.n += inc
            self.ops[eng].append(
                lambda e, f=fn, c=chan, i=inc: f(e).then_inc(c.sem, i))
        else:
            self.ops[eng].append(fn)

    def wait(self, eng, chan, val=None):
        v = chan.n if val is None else val
        if v > 0:
            self.ops[eng].append(lambda e, c=chan, vv=v: e.wait_ge(c.sem, vv))

    def bank_wait(self, key):
        if key in self.bankfree:
            c, v = self.bankfree[key]
            self.wait("tensor", c, v)

    def bank_set(self, key, chan):
        self.bankfree[key] = (chan, chan.n)

    def note_pending(self, chan):
        self.pending.append((chan, chan.n))

    def fence(self, main_chans, tag=""):
        pend = {}
        for c, v in self.pending:
            pend[id(c)] = (c, max(v, pend.get(id(c), (c, 0))[1]))
        self.pending = []
        targets = [(c, c.n) for c in main_chans if c.n > 0]
        targets += list(pend.values())
        for eng in ENGINES:
            for c, v in targets:
                self.wait(eng, c, v)

    def finalize(self, block):
        def runner(name):
            def body(e):
                for fn in self.ops[name]:
                    fn(e)
                if name == "sync":
                    for c in self.chans:
                        if c.n > 0:
                            e.wait_ge(c.sem, c.n)
            return body
        block.sync(runner("sync"))
        block.vector(runner("vector"))
        block.scalar(runner("scalar"))
        block.tensor(runner("tensor"))
        block.gpsimd(runner("gpsimd"))


class Arena:
    def __init__(self, nc, name, kib):
        self.h = nc.alloc_sbuf_tensor(name, [128, kib * 1024], mybir.dt.uint8)
        self.base = self.h.ap()
        self.size = kib * 1024
        self.off = 0

    def carve(self, cols, dtype, n=None):
        esz = {F32: 4, BF16: 2}[dtype]
        out = []
        for _ in range(n or 1):
            b = (self.off + 63) & ~63
            nb = cols * esz
            assert b + nb <= self.size, f"arena overflow {b + nb}>{self.size}"
            out.append(self.base[:, b:b + nb].bitcast(dtype))
            self.off = b + nb
        return out if n else out[0]

    def carve_block(self, cols, dtype, n):
        esz = {F32: 4, BF16: 2}[dtype]
        b = (self.off + 63) & ~63
        nb = cols * esz
        assert b + n * nb <= self.size, "arena overflow (block)"
        whole = self.base[:, b:b + n * nb].bitcast(dtype)
        tiles = [self.base[:, b + i * nb:b + (i + 1) * nb].bitcast(dtype)
                 for i in range(n)]
        self.off = b + n * nb
        return tiles, whole.rearrange("p (n c) -> p n c", n=n)

    def mark(self):
        return self.off

    def reset(self, mark=0):
        self.off = mark


class Stream:
    def __init__(self, K, name, nslots, eng="sync"):
        self.K = K
        self.n = nslots
        self.eng = eng
        self.lch = [K.chan(f"{name}l") for _ in range(nslots)]
        self.i = 0
        self.loaded = []
        self.done = {}

    def issue(self, emits):
        K = self.K
        s = self.i % self.n
        prev = self.i - self.n
        if prev >= 0:
            c, v = self.done[prev]
            K.wait(self.eng, c, v)
        ch = self.lch[s]
        if not isinstance(emits, list):
            emits = [emits]
        for f in emits:
            K.emit(self.eng, f, ch, 16)
        self.loaded.append((s, ch.n))
        self.i += 1
        return self.i - 1

    def wait(self, eng, i):
        s, v = self.loaded[i]
        self.K.wait(eng, self.lch[s], v)

    def set_done(self, i, chan):
        self.done[i] = (chan, chan.n)


class OStream:
    def __init__(self, K, name, nslots):
        self.K = K
        self.n = nslots
        self.sch = [K.chan(f"{name}s") for _ in range(nslots)]
        self.i = 0

    def wait_free(self, eng):
        s = self.i % self.n
        self.K.wait(eng, self.sch[s], self.sch[s].n)

    def store(self, emit, prod_chan):
        K = self.K
        s = self.i % self.n
        K.wait("sync", prod_chan, prod_chan.n)
        K.emit("sync", emit, self.sch[s], 16)
        K.note_pending(self.sch[s])
        self.i += 1


def fmv(dram_ap, p=128):
    return dram_ap.rearrange("(dt p) t -> p dt t", p=p)


def build_nc(dump=(), upto=None):
    K = KB()
    nc = K.nc
    cPE = K.chan("cPE")
    cDV = K.chan("cDV")
    cAC = K.chan("cAC")
    cGP = K.chan("cGP")
    cCC = K.chan("cCC")

    def dram_in(name, shape, dtype):
        kind = "ExternalOutput" if name in dump else "ExternalInput"
        return nc.dram_tensor(name, shape, dtype, kind=kind).ap()

    def scratch(name, shape, dtype):
        kind = "ExternalOutput" if name in dump else "Internal"
        return nc.dram_tensor(name, shape, dtype, kind=kind).ap()

    x0 = dram_in("x0", [D, T], F32)
    wT = {nm: dram_in("w" + nm + "T", [D, D], F32)
          for nm in ("q", "k", "v", "g", "p", "r", "kb", "vb", "o")}
    td_hp = dram_in("td_hp", [16, 128], F32)
    tf_hp = dram_in("tf_hp", [16, 128], F32)
    mix_hp = {nm: dram_in("mix" + nm + "_hp", [16, 128], F32)
              for nm in ("k", "v", "r")}
    gwT_d = dram_in("gwT", [D, 8], F32)
    gb_d = dram_in("gb_row", [1, 8], F32)
    w1T_all = dram_in("w1T_all", [NEXP, D, INTER], BF16)
    w3T_all = dram_in("w3T_all", [NEXP, D, INTER], BF16)
    w2T_all = dram_in("w2T_all", [NEXP, INTER, D], BF16)
    ident_d = dram_in("ident_f", [128, 128], F32)
    triu_d = dram_in("triu_m", [128, 128], F32)
    onec_d = dram_in("ones_col", [128, 1], F32)
    oner_d = dram_in("ones_row", [1, 128], F32)
    sel_d = dram_in("sel_in", [8, 8 * 128], F32)
    umask_d = dram_in("umask_in", [128, 1], F32)
    smask_d = dram_in("smask_in", [128, 1], F32)

    out_fm = nc.dram_tensor("out_fm", [D, T], F32, kind="ExternalOutput").ap()

    q_fm = scratch("q_fm", [D, T], BF16)
    k_fm = scratch("k_fm", [D, T], BF16)
    k_tm = scratch("k_tm", [T, D], BF16)
    v_tm = scratch("v_tm", [T, D], BF16)
    g_fm = scratch("g_fm", [D, T], BF16)
    o_loc = scratch("o_loc", [D, T], BF16)
    og_hbm = scratch("og_hbm", [D, T], BF16)
    r_hbm = scratch("r_hbm", [D, T], BF16)
    xres1 = scratch("xres1", [D, T], F32)
    xres2 = scratch("xres2", [D, T], F32)
    xres3 = scratch("xres3", [D, T], F32)
    dbg_cw = scratch("dbg_cw", [8, T], F32)
    dbg_gs = scratch("dbg_gs", [128, 128], F32)
    dbg_cols = scratch("dbg_cols", [128, 96], F32)
    dbg_kv = scratch("dbg_kv", [D, T], BF16)
    dbg_wkv = scratch("dbg_wkv", [D, T], BF16)
    cc_kv_in = nc.dram_tensor("cc_kv_in", [D, D], BF16).ap()
    cc_kv_out = nc.dram_tensor("cc_kv_out", [D, D], BF16).ap()
    cc_h_in = nc.dram_tensor("cc_h_in", [128, 16], F32).ap()
    cc_h_out = nc.dram_tensor("cc_h_out", [128, 16], F32).ap()
    cc_s_in = nc.dram_tensor("cc_s_in", [128, 16], F32).ap()
    cc_s_out = nc.dram_tensor("cc_s_out", [128, 16], F32).ap()

    cst = Arena(nc, "cst", 16)
    ident = cst.carve(128, F32)
    triu = cst.carve(128, F32)
    onec = cst.carve(1, F32)
    oner = cst.carve(128, F32)
    sel = cst.carve(8 * 128, F32)
    umask = cst.carve(1, F32)
    smask = cst.carve(1, F32)
    gbb = cst.carve(8, F32)
    gbrow_t = cst.carve(8, F32)
    hp_ld = cst.carve(128, F32)
    mixs = {nm: cst.carve(16, F32) for nm in ("k", "v", "r")}
    imixs = {nm: cst.carve(16, F32) for nm in ("k", "v", "r")}
    tds = cst.carve(16, F32)
    tfs = cst.carve(16, F32)
    u_cols = cst.carve(16, F32)
    w_cols = cst.carve(16, F32)
    hprev_u = cst.carve(16, F32)
    s_use = cst.carve(16, F32)
    s_send = cst.carve(16, F32)
    hkeep = cst.carve(16, F32)
    gwTs, gwTsW = cst.carve_block(8, F32, ND)
    mcol = cst.carve(1, F32)
    m2col = cst.carve(1, F32)
    rscol = cst.carve(1, F32)

    R1a = Arena(nc, "R1", 64)
    R1, R1W = R1a.carve_block(1024, F32, ND)
    R1bf = [r.bitcast(BF16) for r in R1]
    R2a = Arena(nc, "R2", 64)
    R2, R2W = R2a.carve_block(2048, BF16, ND)
    R3 = Arena(nc, "R3", 56)

    PBH = [nc.alloc_psum_tensor(f"pb{i}", [128, 512], F32) for i in range(6)]
    PB = [t.ap() for t in PBH]
    PWH = nc.alloc_psum_tensor("pw", [128, 1024], F32)
    PW = PWH.ap()

    pbrot = {"i": 0}

    def next_pb():
        i = pbrot["i"]
        pbrot["i"] = (i + 1) % 6
        return i

    def emv(fn):
        K.wait("vector", cDV, cDV.n)
        K.emit("vector", fn, cDV, 1)

    def mm_group(mms, evacs, key, ev="vector"):
        keys = key if isinstance(key, (list, tuple)) else [key]
        for kk in keys:
            K.bank_wait(kk)
        for m in mms[:-1]:
            K.emit("tensor", m)
        K.emit("tensor", mms[-1], cPE, 1)
        ch = cDV if ev == "vector" else cAC
        K.wait(ev, cPE, cPE.n)
        for f in evacs[:-1]:
            K.emit(ev, f)
        K.emit(ev, evacs[-1], ch, 1)
        for kk in keys:
            K.bank_set(kk, ch)

    c0 = K.chan("c0")
    for dst, src in [
        (ident[:, :], ident_d), (triu[:, :], triu_d),
        (onec[:, 0:1], onec_d), (oner[0:1, :], oner_d),
        (sel[0:8, :], sel_d), (umask[:, 0:1], umask_d),
        (smask[:, 0:1], smask_d), (gbrow_t[0:1, :], gb_d),
        (gwTsW, gwT_d.rearrange("(dt p) c -> p dt c", p=128)),
    ]:
        K.emit("sync", lambda e, d=dst, s=src: e.dma_start(out=d, in_=s),
               c0, 16)
    hp_list = [(td_hp, tds), (tf_hp, tfs)] + \
        [(mix_hp[nm], mixs[nm]) for nm in ("k", "v", "r")]
    K.wait("tensor", c0)
    for src, dst in hp_list:
        K.wait("sync", cPE, cPE.n)
        K.emit("sync", lambda e, s=src: e.dma_start(out=hp_ld[0:16, :],
                                                    in_=s), c0, 16)
        K.wait("tensor", c0, c0.n)
        bk = next_pb()
        mm_group(
            [lambda e, bk=bk: e.transpose(PB[bk][0:128, 0:16],
                                          hp_ld[0:16, :],
                                          ident[0:16, 0:16])],
            [lambda e, bk=bk, d=dst: e.tensor_copy(out=d[:, :],
                                                   in_=PB[bk][0:128, 0:16])],
            bk)
    K.wait("scalar", cDV)
    K.emit("scalar", lambda e: e.activation(out=u_cols[:, :], in_=tds[:, :],
                                            func=AF.Exp), cAC, 1)
    K.emit("scalar", lambda e: e.activation(out=w_cols[:, :], in_=u_cols[:, :],
                                            func=AF.Exp, scale=-1.0), cAC, 1)
    K.wait("vector", c0)
    for nm in ("k", "v", "r"):
        K.emit("vector", lambda e, nm=nm: e.tensor_scalar(
            out=imixs[nm][:, :], in0=mixs[nm][:, :], scalar1=-1.0,
            scalar2=1.0, op0=ALU.mult, op1=ALU.add), cDV, 1)
    bk = next_pb()
    mm_group([lambda e, bk=bk: e.matmul(PB[bk][:, 0:8], oner[0:1, :],
                                        gbrow_t[0:1, :], start=True,
                                        stop=True)],
             [lambda e, bk=bk: e.tensor_copy(out=gbb[:, :],
                                             in_=PB[bk][:, 0:8])], bk)
    fc = {}
    for v in (1.0, EPS_LN, EPS_LN * EPS_RMS, -1e30):
        col = cst.carve(1, F32)
        K.emit("vector", lambda e, col=col, v=v: e.memset(col[:, 0:1], v),
               cDV, 1)
        fc[v] = col
    if "dbg_cols" in dump:
        cdc = K.chan("cdc")
        K.wait("sync", cDV, cDV.n)
        K.wait("sync", cAC, cAC.n)
        for i, tsrc in enumerate([tds, tfs, u_cols, w_cols, mixs["k"],
                                  imixs["k"]]):
            K.emit("sync", lambda e, i=i, t=tsrc: e.dma_start(
                out=dbg_cols[:, i * 16:(i + 1) * 16], in_=t[:, :]), cdc, 16)

    def ln_pass(src_dram, t0, width, out_bf, fused_rms, keep_x=None,
                out_guard=None, gate=False):
        nch = width // 512
        nxs = 1 if width == T else 2
        for eng in ("sync", "vector", "scalar", "tensor", "gpsimd"):
            for ch_ in (cPE, cDV, cAC):
                K.wait(eng, ch_, ch_.n)
        m3 = R3.mark()
        xsl = None if keep_x is not None else R3.carve(width, F32, nxs)
        sqsl = R3.carve(width, F32, nxs)
        mb = R3.carve(width, F32)
        rb = R3.carve(width, F32)
        tmsl = R3.carve(width, F32)
        xnfsl = R3.carve(width, F32, 2) if gate else None
        gate_sc = R3.carve((width // 128) * 8, F32) if gate else None
        regs = [PB[i][0:1, 0:512] for i in range(6)] + \
               [PW[0:1, 0:512], PW[0:1, 512:1024]]
        s1r = regs[:nch]
        s2r = regs[nch:2 * nch]
        bank_keys = list(range(2 * nch)) if nch <= 3 else [0, 1, 2, 3, 4, 5,
                                                           "pw"]
        sq_free = {}
        cx = K.chan("lncx")
        xst = Stream(K, "lnx", nxs) if keep_x is None else None
        for dt in range(ND):
            if keep_x is None:
                it = xst.issue(lambda e, dt=dt: e.dma_start(
                    out=xsl[dt % nxs][:, 0:width],
                    in_=fmv(src_dram)[:, dt, t0:t0 + width]))
                xst.wait("scalar", it)
                xd = xsl[dt % nxs]
            else:
                xd = keep_x[dt]
                K.emit("sync", lambda e, dt=dt, xd=xd: e.dma_start(
                    out=xd[:, 0:width],
                    in_=fmv(src_dram)[:, dt, t0:t0 + width]), cx, 16)
                K.wait("scalar", cx, cx.n)
            sq = sqsl[dt % nxs]
            if dt >= nxs:
                K.wait("scalar", cPE, sq_free[dt - nxs])
            K.emit("scalar", lambda e, sq=sq, xd=xd: e.activation(
                out=sq[:, 0:width], in_=xd[:, 0:width], func=AF.Square),
                cAC, 1)
            if dt == 0:
                for bkk in set(bank_keys):
                    K.bank_wait(bkk)
            if keep_x is None:
                xst.wait("tensor", dt)
            else:
                K.wait("tensor", cx, cx.n)
            K.wait("tensor", cAC, cAC.n)
            for c in range(nch):
                K.emit("tensor", lambda e, r=s1r[c], xd=xd, c=c, dt=dt:
                       e.matmul(r, onec[:, 0:1],
                                xd[:, c * 512:(c + 1) * 512],
                                start=(dt == 0), stop=(dt == ND - 1)))
            for c in range(nch):
                K.emit("tensor", lambda e, r=s2r[c], sq=sq, c=c, dt=dt:
                       e.matmul(r, onec[:, 0:1],
                                sq[:, c * 512:(c + 1) * 512],
                                start=(dt == 0), stop=(dt == ND - 1)),
                       cPE if c == nch - 1 else None, 1)
            sq_free[dt] = cPE.n
            if keep_x is None:
                xst.set_done(dt, cPE)
        rm = mb[0:1, 0:width]
        re2 = rb[0:1, 0:width]
        rv = tmsl[0:1, 0:width]
        K.wait("vector", cPE, cPE.n)
        for c in range(nch):
            K.emit("vector", lambda e, c=c, rm=rm: e.tensor_single_scalar(
                out=rm[:, c * 512:(c + 1) * 512], in_=s1r[c],
                scalar=1.0 / D, op=ALU.mult), cDV, 1)
            K.emit("vector", lambda e, c=c, re2=re2: e.tensor_single_scalar(
                out=re2[:, c * 512:(c + 1) * 512], in_=s2r[c],
                scalar=1.0 / D, op=ALU.mult), cDV, 1)
        for bkk in set(bank_keys):
            K.bank_set(bkk, cDV)
        K.emit("vector", lambda e: e.tensor_tensor(
            out=rv, in0=rm, in1=rm, op=ALU.mult), cDV, 1)
        K.emit("vector", lambda e: e.tensor_tensor(
            out=rv, in0=re2, in1=rv, op=ALU.subtract), cDV, 1)
        if fused_rms:
            K.emit("vector", lambda e: e.tensor_single_scalar(
                out=rv, in_=rv, scalar=1.0 + EPS_RMS, op=ALU.mult), cDV, 1)
        K.wait("scalar", cDV)
        bcol = fc[EPS_LN * EPS_RMS] if fused_rms else fc[EPS_LN]
        K.emit("scalar", lambda e, bcol=bcol: e.activation(
            out=re2, in_=rv, func=AF.Ln, bias=bcol[0:1, 0:1]), cAC, 1)
        K.emit("scalar", lambda e: e.activation(
            out=re2, in_=re2, func=AF.Exp, scale=-0.5), cAC, 1)
        K.wait("tensor", cAC)
        for c in range(nch):
            for rsrc, dstb in ((rm, mb), (re2, rb)):
                bk = next_pb()
                mm_group(
                    [lambda e, bk=bk, rsrc=rsrc, c=c: e.matmul(
                        PB[bk][:, 0:512], oner[0:1, :],
                        rsrc[:, c * 512:(c + 1) * 512], start=True,
                        stop=True)],
                    [lambda e, bk=bk, dstb=dstb, c=c: e.tensor_copy(
                        out=dstb[:, c * 512:(c + 1) * 512],
                        in_=PB[bk][:, 0:512])], bk)
        gk = None
        gate_free = {}
        if gate:
            gk = next_pb()
            K.bank_wait(gk)
        if out_guard is not None:
            K.wait("vector", out_guard[0], out_guard[1])
        xst2 = Stream(K, "lnx2", nxs) if keep_x is None else None
        if keep_x is None:
            K.wait("sync", cPE, cPE.n)
        for dt in range(ND):
            if keep_x is None:
                it = xst2.issue(lambda e, dt=dt: e.dma_start(
                    out=xsl[dt % nxs][:, 0:width],
                    in_=fmv(src_dram)[:, dt, t0:t0 + width]))
                xst2.wait("vector", it)
                xd = xsl[dt % nxs]
            else:
                xd = keep_x[dt]
            K.wait("vector", cDV, cDV.n)
            K.emit("vector", lambda e, xd=xd: e.tensor_tensor(
                out=tmsl[:, 0:width], in0=xd[:, 0:width],
                in1=mb[:, 0:width], op=ALU.subtract))
            if gate:
                xnf = xnfsl[dt % 2]
                if dt >= 2:
                    K.wait("vector", cPE, gate_free[dt - 2])
                K.emit("vector", lambda e, xnf=xnf: e.tensor_tensor(
                    out=xnf[:, 0:width], in0=tmsl[:, 0:width],
                    in1=rb[:, 0:width], op=ALU.mult), cDV, 1)
                K.emit("vector", lambda e, o=out_bf[dt], xnf=xnf:
                       e.tensor_copy(out=o[:, 0:width],
                                     in_=xnf[:, 0:width]), cDV, 1)
                K.wait("tensor", cDV, cDV.n)
                nsub = width // 128
                for ts_ in range(nsub):
                    K.emit("tensor", lambda e, xnf=xnf, ts_=ts_, dt=dt, gk=gk:
                           e.matmul(PB[gk][:, ts_ * 8:(ts_ + 1) * 8],
                                    xnf[:, ts_ * 128:(ts_ + 1) * 128],
                                    gwTs[dt][0:128, 0:8],
                                    start=(dt == 0 and ts_ == 0),
                                    stop=(dt == ND - 1)),
                           cPE if ts_ == nsub - 1 else None, 1)
                gate_free[dt] = cPE.n
            else:
                K.emit("vector", lambda e, o=out_bf[dt]: e.tensor_tensor(
                    out=o[:, 0:width], in0=tmsl[:, 0:width],
                    in1=rb[:, 0:width], op=ALU.mult), cDV, 1)
            if keep_x is None:
                xst2.set_done(dt, cDV)
        if gate:
            K.wait("vector", cPE, cPE.n)
            K.emit("vector", lambda e, gk=gk: e.tensor_copy(
                out=gate_sc[:, :], in_=PB[gk][:, 0:(width // 128) * 8]),
                cDV, 1)
            K.bank_set(gk, cDV)
            return gate_sc, m3
        R3.reset(m3)
        return None, m3

    ln_pass(x0, 0, T, R2, True)

    K.fence((cPE, cDV, cAC, cGP), "p2")
    m3p2 = R3.mark()
    wcf, wcfW = [], []
    wcb, wcbW = [], []
    for i in range(2):
        t_, w_ = R3.carve_block(128, F32, ND)
        wcf.append(t_)
        wcfW.append(w_)
        t_, w_ = R3.carve_block(128, BF16, ND)
        wcb.append(t_)
        wcbW.append(w_)
    stg2 = R3.carve(2048, BF16, 2)
    tmst_t, tmst_W = [], []
    for i in range(2):
        t_, w_ = R3.carve_block(128, BF16, ND)
        tmst_t.append(t_)
        tmst_W.append(w_)
    wst_p2 = Stream(K, "p2w", 2)
    ost_fm = OStream(K, "pfo", 2)
    ost_tm = OStream(K, "pto", 2)
    ccast2 = K.chan("pfc")
    bfree_p2 = {}

    def proj_a(wdram, act, src_tiles, fm_out=None, tm_out=None):
        for ot in range(ND):
            sl = ot % 2
            it = wst_p2.issue(lambda e, sl=sl, w=wdram, ot=ot: e.dma_start(
                out=wcfW[sl], in_=fmv(w)[:, :, ot * 128:(ot + 1) * 128]))
            wst_p2.wait("vector", it)
            if sl in bfree_p2:
                K.wait("vector", cPE, bfree_p2[sl])
            K.emit("vector", lambda e, sl=sl: e.tensor_copy(
                out=wcbW[sl], in_=wcfW[sl]), ccast2, 1)
            wst_p2.set_done(it, cDV)
            K.wait("tensor", ccast2, ccast2.n)
            if fm_out is not None:
                st = stg2[ot % 2]
                ost_fm.wait_free("scalar")
                for c in range(4):
                    bk = next_pb()
                    mms = [(lambda e, bk=bk, sl=sl, dt=dt, c=c: e.matmul(
                        PB[bk][:, 0:512], wcb[sl][dt],
                        src_tiles[dt][:, c * 512:(c + 1) * 512],
                        start=(dt == 0), stop=(dt == ND - 1)))
                        for dt in range(ND)]
                    evs = [(lambda e, bk=bk, st=st, c=c: e.activation(
                        out=st[:, c * 512:(c + 1) * 512],
                        in_=PB[bk][:, 0:512], func=act))]
                    mm_group(mms, evs, bk, ev="scalar")
                ost_fm.store(lambda e, st=st, o=fm_out, ot=ot:
                             e.dma_start(out=o[ot * 128:(ot + 1) * 128, :],
                                         in_=st[:, :]), cAC)
            if tm_out is not None:
                tst = tmst_t[ot % 2]
                tw = tmst_W[ot % 2]
                ost_tm.wait_free("scalar")
                for tt in range(ND):
                    bk = next_pb()
                    mms = [(lambda e, bk=bk, sl=sl, dt=dt, tt=tt: e.matmul(
                        PB[bk][:, 0:128],
                        src_tiles[dt][:, tt * 128:(tt + 1) * 128],
                        wcb[sl][dt][:, 0:128],
                        start=(dt == 0), stop=(dt == ND - 1)))
                        for dt in range(ND)]
                    evs = [(lambda e, bk=bk, tst=tst, tt=tt: e.activation(
                        out=tst[tt][:, :], in_=PB[bk][:, 0:128], func=act))]
                    mm_group(mms, evs, bk, ev="scalar")
                ost_tm.store(lambda e, tw=tw, o=tm_out, ot=ot: e.dma_start(
                    out=o.rearrange("(tt p) o -> p tt o", p=128)
                    [:, :, ot * 128:(ot + 1) * 128], in_=tw), cAC)
            bfree_p2[sl] = cPE.n

    proj_a(wT["q"], AF.Silu, R2, fm_out=q_fm)
    proj_a(wT["k"], AF.Silu, R2, fm_out=k_fm, tm_out=k_tm)
    proj_a(wT["v"], AF.Silu, R2, tm_out=v_tm)
    proj_a(wT["g"], AF.Sigmoid, R2, fm_out=g_fm)
    R3.reset(m3p2)
    if upto == "proj_a":
        return K

    K.fence((cPE, cDV, cAC, cGP), "p3")
    m3p3 = R3.mark()
    qc_t, qc_W = [], []
    for i in range(2):
        t_, w_ = R3.carve_block(ACH, BF16, ND)
        qc_t.append(t_)
        qc_W.append(w_)
    kc_t, kc_W = R3.carve_block(ACH, BF16, ND)
    kt_t, kt_W = R3.carve_block(2048, BF16, 2)
    vt_t, vt_W = R3.carve_block(1024, BF16, 2)
    sT = []
    for cch in range(NACH):
        t_, _ = R3.carve_block(ACH, BF16, 2)
        sT.append(t_)
    ol_t0, ol_W0 = R3.carve_block(ACH, BF16, 8)
    ol_t = [ol_t0, ol_t0]
    ol_W = [ol_W0, ol_W0]
    snd = R3.carve(1024, BF16, 2)

    for cch in range(NACH):
        K.emit("gpsimd", lambda e, cch=cch: e.memset(
            sT[cch][1][:, 0:128], 0.0), cGP, 1)
    sT_ready = 0
    qs = Stream(K, "p3q", 2)
    ks = Stream(K, "p3k", 1)
    kts = Stream(K, "p3kt", 1)
    vs = Stream(K, "p3v", 1)
    oss = OStream(K, "p3o", 1)
    sndst = OStream(K, "p3snd", 2)
    for sp in range(2):
        K.wait("gpsimd", cPE, cPE.n)
        K.wait("gpsimd", cDV, cDV.n)
        K.wait("gpsimd", cAC, cAC.n)
        for dt in range(ND):
            K.emit("gpsimd", lambda e, dt=dt: e.memset(R1[dt][:, :], 0.0),
                   cGP, 1)
            K.emit("gpsimd", lambda e, dt=dt: e.memset(
                R2[dt][:, 0:1024], 0.0), cGP, 1)
        K.wait("sync", cPE, cPE.n)
        K.wait("sync", cDV, cDV.n)
        kvev = {}
        for cch in range(NACH):
            t0c = cch * ACH
            sl = cch % 2
            iq = qs.issue(lambda e, sl=sl, t0c=t0c: e.dma_start(
                out=qc_W[sl], in_=fmv(q_fm)[:, :, t0c:t0c + ACH]))
            if sp == 0:
                ik = ks.issue(lambda e, t0c=t0c: e.dma_start(
                    out=kc_W, in_=fmv(k_fm)[:, :, t0c:t0c + ACH]))
            ikt = kts.issue(lambda e, cch=cch: e.dma_start(
                out=kt_W, in_=k_tm.rearrange("(j p) o -> p j o", p=128)
                [:, 2 * cch:2 * cch + 2, :]))
            iv = vs.issue(lambda e, cch=cch, sp=sp: e.dma_start(
                out=vt_W, in_=v_tm.rearrange("(j p) o -> p j o", p=128)
                [:, 2 * cch:2 * cch + 2, sp * 1024:(sp + 1) * 1024]))
            if cch == 0:
                K.wait("tensor", cGP, cGP.n)
            if sp == 0:
                ks.wait("tensor", ik)
                qs.wait("tensor", iq)
                for jt in range(2):
                    nn = ACH - jt * 128
                    bk = 4 + jt
                    mms = [(lambda e, bk=bk, dt=dt, jt=jt, nn=nn, sl=sl:
                            e.matmul(
                                PB[bk][:, 0:nn],
                                kc_t[dt][:, jt * 128:(jt + 1) * 128],
                                qc_t[sl][dt][:, jt * 128:jt * 128 + nn],
                                start=(dt == 0), stop=(dt == ND - 1)))
                           for dt in range(ND)]
                    evs = [(lambda e, bk=bk, cch=cch, jt=jt: e.tensor_tensor(
                        out=sT[cch][jt][:, jt * 128:jt * 128 + 128],
                        in0=PB[bk][:, 0:128], in1=triu[:, :], op=ALU.mult))]
                    if jt == 0:
                        evs.append((lambda e, bk=bk, cch=cch: e.tensor_copy(
                            out=sT[cch][0][:, 128:256],
                            in_=PB[bk][:, 128:256])))
                    mm_group(mms, evs, bk)
                ks.set_done(ik, cPE)
                K.wait("tensor", cDV, cDV.n)
            else:
                qs.wait("tensor", iq)
                if cch == 0:
                    K.wait("tensor", cDV, sT_ready)
            vs.wait("tensor", iv)
            K.wait("tensor", cAC, cAC.n)
            ol = ol_t[sl]
            oss.wait_free("vector")
            for dv in range(8):
                bk = dv % 4
                mms = []
                for dt in range(ND):
                    mms.append(lambda e, bk=bk, dt=dt, dv=dv, sl=sl:
                               e.matmul(
                                   PB[bk][:, 0:ACH],
                                   R2[dt][:, dv * 128:(dv + 1) * 128],
                                   qc_t[sl][dt][:, 0:ACH],
                                   start=(dt == 0), stop=False))
                for jt in range(2):
                    mms.append(lambda e, bk=bk, jt=jt, dv=dv, cch=cch:
                               e.matmul(
                                   PB[bk][:, 0:ACH],
                                   vt_t[jt][:, dv * 128:(dv + 1) * 128],
                                   sT[cch][jt][:, 0:ACH],
                                   start=False, stop=(jt == 1)))
                evs = [(lambda e, bk=bk, ol=ol, dv=dv: e.tensor_copy(
                    out=ol[dv][:, :], in_=PB[bk][:, 0:ACH]))]
                mm_group(mms, evs, bk)
            qs.set_done(iq, cPE)
            oss.store(lambda e, w=ol_W[sl], cch=cch, sp=sp: e.dma_start(
                out=o_loc.rearrange("(dv p) t -> p dv t", p=128)
                [:, sp * 8:(sp + 1) * 8, cch * ACH:(cch + 1) * ACH],
                in_=w), cDV)
            kts.wait("tensor", ikt)
            for dt in range(ND):
                reg = dt % 2
                keys = ["pw"] if reg == 0 else [4, 5]
                for kk in keys:
                    K.bank_wait(kk)
                for jt in range(2):
                    for nh in range(2):
                        lastm = (jt == 1 and nh == 1)
                        if reg == 0:
                            K.emit("tensor", lambda e, jt=jt, nh=nh, dt=dt:
                                   e.matmul(
                                       PW[:, nh * 512:(nh + 1) * 512],
                                       kt_t[jt][:, dt * 128:(dt + 1) * 128],
                                       vt_t[jt][:, nh * 512:(nh + 1) * 512],
                                       start=(jt == 0), stop=(jt == 1)),
                                   cPE if lastm else None, 1)
                        else:
                            K.emit("tensor", lambda e, jt=jt, nh=nh, dt=dt:
                                   e.matmul(
                                       PB[4 + nh][:, 0:512],
                                       kt_t[jt][:, dt * 128:(dt + 1) * 128],
                                       vt_t[jt][:, nh * 512:(nh + 1) * 512],
                                       start=(jt == 0), stop=(jt == 1)),
                                   cPE if lastm else None, 1)
                K.wait("vector", cPE, cPE.n)
                K.wait("vector", cAC, kvev.get(dt, 0))
                if reg == 0:
                    K.emit("vector", lambda e, dt=dt: e.tensor_tensor(
                        out=R1[dt][:, :], in0=PW[:, 0:1024],
                        in1=R1[dt][:, :], op=ALU.add), cDV, 1)
                else:
                    K.emit("vector", lambda e, dt=dt: e.tensor_tensor(
                        out=R1[dt][:, 0:512], in0=PB[4][:, 0:512],
                        in1=R1[dt][:, 0:512], op=ALU.add))
                    K.emit("vector", lambda e, dt=dt: e.tensor_tensor(
                        out=R1[dt][:, 512:1024], in0=PB[5][:, 0:512],
                        in1=R1[dt][:, 512:1024], op=ALU.add), cDV, 1)
                for kk in keys:
                    K.bank_set(kk, cDV)
                if cch < NACH - 1:
                    K.wait("scalar", cDV, cDV.n)
                    K.emit("scalar", lambda e, dt=dt: e.copy(
                        out=R2[dt][:, 0:1024], in_=R1[dt][:, :]), cAC, 1)
                    kvev[dt] = cAC.n
            kts.set_done(ikt, cPE)
            vs.set_done(iv, cPE)
        if sp == 0:
            sT_ready = cDV.n
        for dt in range(ND):
            sndst.wait_free("vector")
            sslot = snd[dt % 2]
            K.emit("vector", lambda e, dt=dt, s=sslot: e.tensor_single_scalar(
                out=s[:, :], in_=R1[dt][:, :], scalar=smask[:, 0:1],
                op=ALU.mult), cDV, 1)
            sndst.store(lambda e, dt=dt, sp=sp, s=sslot: e.dma_start(
                out=cc_kv_in[dt * 128:(dt + 1) * 128,
                             sp * 1024:(sp + 1) * 1024],
                in_=s[:, :]), cDV)
        for ch in sndst.sch:
            K.wait("gpsimd", ch, ch.n)
    R3.reset(m3p3)

    K.emit("gpsimd", lambda e: e.collective_compute(
        "AllReduce", ALU.add, replica_groups=[[0, 1], [2, 3], [4, 5], [6, 7]],
        ins=[cc_kv_in], outs=[cc_kv_out]), cCC, 1)
    if upto == "scan_a":
        return K

    K.fence((cPE, cDV, cAC, cGP), "p6a")
    m3p6 = R3.mark()
    carr_t, carr_W = [], []
    for i in range(2):
        t_, w_ = R3.carve_block(128, BF16, ND)
        carr_t.append(t_)
        carr_W.append(w_)
    olc = R3.carve(2048, BF16, 2)
    gc = R3.carve(2048, BF16, 2)
    ogst = R3.carve(2048, BF16, 2)
    cq6 = K.chan("cq6")
    K.emit("sync", lambda e: e.dma_start(out=R2W, in_=fmv(q_fm)), cq6, 16)
    K.wait("sync", cCC, cCC.n)
    cs6 = Stream(K, "p6c", 2)
    ls6 = Stream(K, "p6l", 2)
    gs6 = Stream(K, "p6g", 2)
    ost6 = OStream(K, "p6o", 2)
    K.wait("tensor", cq6, cq6.n)
    for dv in range(ND):
        sl = dv % 2
        ic = cs6.issue(lambda e, sl=sl, dv=dv: e.dma_start(
            out=carr_W[sl],
            in_=fmv(cc_kv_out)[:, :, dv * 128:(dv + 1) * 128]))
        il = ls6.issue(lambda e, sl=sl, dv=dv: e.dma_start(
            out=olc[sl][:, :], in_=o_loc[dv * 128:(dv + 1) * 128, :]))
        ig = gs6.issue(lambda e, sl=sl, dv=dv: e.dma_start(
            out=gc[sl][:, :], in_=g_fm[dv * 128:(dv + 1) * 128, :]))
        cs6.wait("tensor", ic)
        ls6.wait("vector", il)
        gs6.wait("vector", ig)
        st = ogst[dv % 2]
        ost6.wait_free("vector")
        for c in range(4):
            bk = next_pb()
            mms = [(lambda e, bk=bk, sl=sl, dt=dt, c=c: e.matmul(
                PB[bk][:, 0:512], carr_t[sl][dt],
                R2[dt][:, c * 512:(c + 1) * 512],
                start=(dt == 0), stop=(dt == ND - 1)))
                for dt in range(ND)]
            evs = [
                (lambda e, bk=bk, sl=sl, c=c, st=st: e.scalar_tensor_tensor(
                    out=st[:, c * 512:(c + 1) * 512], in0=PB[bk][:, 0:512],
                    scalar=umask[:, 0:1],
                    in1=olc[sl][:, c * 512:(c + 1) * 512],
                    op0=ALU.mult, op1=ALU.add)),
                (lambda e, sl=sl, c=c, st=st: e.tensor_tensor(
                    out=st[:, c * 512:(c + 1) * 512],
                    in0=st[:, c * 512:(c + 1) * 512],
                    in1=gc[sl][:, c * 512:(c + 1) * 512], op=ALU.mult)),
            ]
            mm_group(mms, evs, bk)
        cs6.set_done(ic, cPE)
        ls6.set_done(il, cDV)
        gs6.set_done(ig, cDV)
        ost6.store(lambda e, st=st, dv=dv: e.dma_start(
            out=og_hbm[dv * 128:(dv + 1) * 128, :], in_=st[:, :]), cDV)

    K.fence((cPE, cDV, cAC, cGP), "p6b")
    R3.reset(m3p6)
    cog6 = K.chan("cog6")
    K.emit("sync", lambda e: e.dma_start(out=R2W, in_=fmv(og_hbm)), cog6, 16)
    wp6 = Stream(K, "p6w", 2)
    x06 = Stream(K, "p6x", 2)
    xst6 = OStream(K, "p6y", 2)
    wpbt, wpbW = [], []
    wpft, wpfW = [], []
    for i in range(2):
        t_, w_ = R3.carve_block(128, BF16, ND)
        wpbt.append(t_)
        wpbW.append(w_)
        t_, w_ = R3.carve_block(128, F32, ND)
        wpft.append(t_)
        wpfW.append(w_)
    x0sl = R3.carve(2048, F32, 2)
    x1st = R3.carve(2048, F32, 2)
    ccast6 = K.chan("cc6")
    K.wait("tensor", cog6, cog6.n)
    bfree6 = {}
    for ht in range(ND):
        sl = ht % 2
        iw = wp6.issue(lambda e, sl=sl, ht=ht: e.dma_start(
            out=wpfW[sl], in_=fmv(wT["p"])[:, :, ht * 128:(ht + 1) * 128]))
        ix = x06.issue(lambda e, sl=sl, ht=ht: e.dma_start(
            out=x0sl[sl][:, :], in_=fmv(x0)[:, ht, :]))
        wp6.wait("vector", iw)
        if sl in bfree6:
            K.wait("vector", cPE, bfree6[sl])
        K.emit("vector", lambda e, sl=sl: e.tensor_copy(
            out=wpbW[sl], in_=wpfW[sl]), ccast6, 1)
        wp6.set_done(iw, cDV)
        x06.wait("vector", ix)
        st = x1st[ht % 2]
        xst6.wait_free("vector")
        K.wait("tensor", ccast6, ccast6.n)
        for c in range(4):
            bk = next_pb()
            mms = [(lambda e, bk=bk, sl=sl, dvv=dvv, c=c: e.matmul(
                PB[bk][:, 0:512], wpbt[sl][dvv],
                R2[dvv][:, c * 512:(c + 1) * 512],
                start=(dvv == 0), stop=(dvv == ND - 1)))
                for dvv in range(ND)]
            evs = [(lambda e, bk=bk, sl=sl, c=c, st=st: e.tensor_tensor(
                out=st[:, c * 512:(c + 1) * 512], in0=PB[bk][:, 0:512],
                in1=x0sl[sl][:, c * 512:(c + 1) * 512], op=ALU.add))]
            mm_group(mms, evs, bk)
        bfree6[sl] = cPE.n
        x06.set_done(ix, cDV)
        xst6.store(lambda e, st=st, ht=ht: e.dma_start(
            out=xres1[ht * 128:(ht + 1) * 128, :], in_=st[:, :]), cDV)
    R3.reset(m3p6)
    if upto == "x1":
        return K

    ICH = [(0, 256), (256, 256), (512, 256), (768, 256), (1024, 256),
           (1280, 128)]

    def ffn(src_dram, dst_dram, tag):
        K.fence((cPE, cDV, cAC, cGP), tag)
        m3ffn = R3.mark()
        cwrows = R3.carve(PAN, F32)
        for pan in range(2):
            t0 = pan * PAN
            gate_sc, m3f = ln_pass(
                src_dram, t0, PAN,
                [R2[dt][:, 0:PAN] for dt in range(ND)], False,
                keep_x=R1, out_guard=(cPE, cPE.n), gate=True)
            gt = R3.carve(8, F32, 4)
            for ts_ in range(8):
                s = gate_sc[:, ts_ * 8:(ts_ + 1) * 8]
                K.wait("scalar", cDV, cDV.n)
                K.emit("scalar", lambda e, s=s: e.activation(
                    out=gt[0][:, 0:8], in_=s, func=AF.Abs), cAC, 1)
                K.emit("scalar", lambda e: e.activation(
                    out=gt[1][:, 0:8], in_=gt[0][:, 0:8], func=AF.Exp,
                    scale=-1.0), cAC, 1)
                K.emit("scalar", lambda e: e.activation(
                    out=gt[1][:, 0:8], in_=gt[1][:, 0:8], func=AF.Ln,
                    bias=fc[1.0][:, 0:1]), cAC, 1)
                K.emit("scalar", lambda e, s=s: e.activation(
                    out=gt[2][:, 0:8], in_=s, func=AF.Relu), cAC, 1)
                K.wait("vector", cAC, cAC.n)
                K.emit("vector", lambda e: e.tensor_tensor(
                    out=gt[0][:, 0:8], in0=gt[1][:, 0:8], in1=gt[2][:, 0:8],
                    op=ALU.add), cDV, 1)
                K.wait("scalar", cDV, cDV.n)
                K.emit("scalar", lambda e: e.activation(
                    out=gt[1][:, 0:8], in_=gt[0][:, 0:8], func=AF.Ln),
                    cAC, 1)
                K.emit("scalar", lambda e: e.activation(
                    out=gt[1][:, 0:8], in_=gt[1][:, 0:8], func=AF.Exp,
                    scale=0.5), cAC, 1)
                K.wait("vector", cAC, cAC.n)
                emv(lambda e: e.tensor_tensor(
                    out=gt[2][:, 0:8], in0=gt[1][:, 0:8], in1=gbb[:, :],
                    op=ALU.add))
                emv(lambda e: e.reduce_max(
                    mcol[:, 0:1], gt[2][:, 0:8], axis=AXX))
                emv(lambda e: e.tensor_single_scalar(
                    out=gt[3][:, 0:8], in_=gt[2][:, 0:8],
                    scalar=mcol[:, 0:1], op=ALU.is_ge))
                emv(lambda e: e.scalar_tensor_tensor(
                    out=gt[3][:, 0:8], in0=gt[3][:, 0:8],
                    scalar=fc[-1e30][:, 0:1],
                    in1=gt[2][:, 0:8], op0=ALU.mult, op1=ALU.add))
                emv(lambda e: e.reduce_max(
                    m2col[:, 0:1], gt[3][:, 0:8], axis=AXX))
                emv(lambda e: e.tensor_single_scalar(
                    out=gt[3][:, 0:8], in_=gt[2][:, 0:8],
                    scalar=m2col[:, 0:1], op=ALU.is_ge))
                emv(lambda e: e.tensor_tensor(
                    out=gt[0][:, 0:8], in0=gt[1][:, 0:8], in1=gt[3][:, 0:8],
                    op=ALU.mult))
                bk = next_pb()
                K.wait("tensor", cDV, cDV.n)
                mm_group(
                    [lambda e, bk=bk: e.transpose(
                        PB[bk][0:8, 0:128], gt[0][:, 0:8], ident[:, :])],
                    [lambda e, bk=bk, ts_=ts_: e.tensor_copy(
                        out=cwrows[0:8, ts_ * 128:(ts_ + 1) * 128],
                        in_=PB[bk][0:8, 0:128])], bk)
            if tag == "f1":
                cdbg = K.chan("cdbg")
                K.wait("sync", cDV, cDV.n)
                K.emit("sync", lambda e, pan=pan: e.dma_start(
                    out=dbg_cw[0:8, pan * PAN:(pan + 1) * PAN],
                    in_=cwrows[0:8, :]), cdbg, 16)
                K.emit("sync", lambda e, pan=pan: e.dma_start(
                    out=dbg_gs[:, pan * 64:(pan + 1) * 64],
                    in_=gate_sc[:, :]), cdbg, 16)
            R3.reset(m3f)
            w13f, w13fW = [], []
            for i in range(2):
                t1_, w1_ = R3.carve_block(256, BF16, ND)
                t3_, w3_ = R3.carve_block(256, BF16, ND)
                w13f.append((t1_, t3_))
                w13fW.append((w1_, w3_))
            w2sl, w2slW = [], []
            for i in range(2):
                t_, w_ = R3.carve_block(128, BF16, 11)
                w2sl.append(t_)
                w2slW.append(w_)
            cwb = R3.carve(PAN, F32)
            s1sl = R3.carve(512, F32, 2)
            hcsl = R3.carve(512, F32, 2)
            w13s = Stream(K, "fw13", 2)
            w2s = Stream(K, "fw2", 2)
            K.wait("sync", cPE, cPE.n)
            K.wait("sync", cDV, cDV.n)
            for e_ in range(NEXP):
                if e_ < 8:
                    K.wait("tensor", cDV, cDV.n)
                    for half in range(2):
                        bk = next_pb()
                        mm_group(
                            [lambda e, bk=bk, e_=e_, half=half: e.matmul(
                                PB[bk][:, 0:512],
                                sel[0:8, e_ * 128:(e_ + 1) * 128],
                                cwrows[0:8, half * 512:(half + 1) * 512],
                                start=True, stop=True)],
                            [lambda e, bk=bk, half=half: e.tensor_copy(
                                out=cwb[:, half * 512:(half + 1) * 512],
                                in_=PB[bk][:, 0:512])], bk)
                for ici, (i0, iw) in enumerate(ICH):
                    sl = ici % 2
                    it13 = w13s.issue([
                        (lambda e, sl=sl, e_=e_, i0=i0, iw=iw: e.dma_start(
                            out=w13fW[sl][0][:, :, 0:iw],
                            in_=fmv(w1T_all[e_])[:, :, i0:i0 + iw])),
                        (lambda e, sl=sl, e_=e_, i0=i0, iw=iw: e.dma_start(
                            out=w13fW[sl][1][:, :, 0:iw],
                            in_=fmv(w3T_all[e_])[:, :, i0:i0 + iw]))])
                    w13s.wait("tensor", it13)
                    for it in range(iw // 128):
                        itg = i0 // 128 + it
                        for tc in range(2):
                            bk1 = next_pb()
                            mms1 = [(lambda e, bk1=bk1, sl=sl, dt=dt, it=it,
                                     tc=tc: e.matmul(
                                         PB[bk1][:, 0:512],
                                         w13f[sl][0][dt]
                                         [:, it * 128:(it + 1) * 128],
                                         R2[dt][:, tc * 512:(tc + 1) * 512],
                                         start=(dt == 0),
                                         stop=(dt == ND - 1)))
                                    for dt in range(ND)]
                            ss = s1sl[tc]
                            evs1 = [(lambda e, bk1=bk1, ss=ss: e.activation(
                                out=ss[:, 0:512], in_=PB[bk1][:, 0:512],
                                func=AF.Silu))]
                            mm_group(mms1, evs1, bk1, ev="scalar")
                            bk3 = next_pb()
                            mms3 = [(lambda e, bk3=bk3, sl=sl, dt=dt, it=it,
                                     tc=tc: e.matmul(
                                         PB[bk3][:, 0:512],
                                         w13f[sl][1][dt]
                                         [:, it * 128:(it + 1) * 128],
                                         R2[dt][:, tc * 512:(tc + 1) * 512],
                                         start=(dt == 0),
                                         stop=(dt == ND - 1)))
                                    for dt in range(ND)]
                            hdst = R2[itg][:, 1024 + tc * 512:
                                           1024 + (tc + 1) * 512]
                            hc = hcsl[tc]
                            K.wait("vector", cAC, cAC.n)
                            if e_ < 8:
                                evs3 = [
                                    (lambda e, bk3=bk3, hc=hc, tc=tc:
                                     e.tensor_tensor(
                                         out=hc[:, 0:512],
                                         in0=PB[bk3][:, 0:512],
                                         in1=cwb[:, tc * 512:(tc + 1) * 512],
                                         op=ALU.mult)),
                                    (lambda e, hc=hc, ss=ss, hdst=hdst:
                                     e.tensor_tensor(
                                         out=hdst, in0=ss[:, 0:512],
                                         in1=hc[:, 0:512], op=ALU.mult)),
                                ]
                            else:
                                evs3 = [(lambda e, bk3=bk3, ss=ss, hdst=hdst:
                                         e.tensor_tensor(
                                             out=hdst, in0=ss[:, 0:512],
                                             in1=PB[bk3][:, 0:512],
                                             op=ALU.mult))]
                            mm_group(mms3, evs3, bk3)
                    w13s.set_done(it13, cPE)
                K.wait("tensor", cDV, cDV.n)
                for dout in range(ND):
                    sl = dout % 2
                    iw2 = w2s.issue(lambda e, sl=sl, e_=e_, dout=dout:
                                    e.dma_start(
                                        out=w2slW[sl],
                                        in_=w2T_all[e_].rearrange(
                                            "(it p) d -> p it d", p=128)
                                        [:, :, dout * 128:(dout + 1) * 128]))
                    w2s.wait("tensor", iw2)
                    for tc in range(2):
                        bk = next_pb()
                        mms = [(lambda e, bk=bk, sl=sl, it=it, tc=tc:
                                e.matmul(
                                    PB[bk][:, 0:512], w2sl[sl][it],
                                    R2[it][:, 1024 + tc * 512:
                                           1024 + (tc + 1) * 512],
                                    start=(it == 0), stop=(it == 10)))
                               for it in range(11)]
                        evs = [(lambda e, bk=bk, dout=dout, tc=tc:
                                e.tensor_tensor(
                                    out=R1[dout][:, tc * 512:(tc + 1) * 512],
                                    in0=PB[bk][:, 0:512],
                                    in1=R1[dout][:, tc * 512:(tc + 1) * 512],
                                    op=ALU.add))]
                        mm_group(mms, evs, bk)
                    w2s.set_done(iw2, cPE)
            cyst = K.chan(tag + "y")
            K.wait("sync", cDV, cDV.n)
            K.emit("sync", lambda e, t0=t0: e.dma_start(
                out=fmv(dst_dram)[:, :, t0:t0 + PAN], in_=R1W), cyst, 16)
            K.note_pending(cyst)
            K.wait("sync", cyst, cyst.n)
            R3.reset(m3f)
        R3.reset(m3ffn)

    ffn(xres1, xres2, "f1")
    if upto == "x2":
        return K

    K.fence((cPE, cDV, cAC, cGP), "p8")
    m38 = R3.mark()
    xcols = R3.carve(64, F32)
    hl_t = R3.carve(16, F32)
    rb8 = R3.carve(4, F32)
    c8 = K.chan("c8")
    xcv = xcols[:, :].rearrange("p (dt c) -> p dt c", dt=16)
    K.emit("sync", lambda e: e.dma_start(
        out=xcv, in_=fmv(xres2)[:, :, T - 4:T]), c8, 16)
    K.wait("scalar", c8)
    K.emit("scalar", lambda e: e.activation(
        out=hl_t[:, :], in_=xcv[:, :, 3].opt(), func=AF.Square), cAC, 1)
    K.wait("tensor", c8)
    K.wait("tensor", cAC, cAC.n)
    bkA = next_pb()
    bkB = next_pb()
    K.bank_wait(bkA)
    K.bank_wait(bkB)
    for dt in range(ND):
        K.emit("tensor", lambda e, dt=dt, bkA=bkA: e.matmul(
            PB[bkA][0:1, 0:1], onec[:, 0:1], xcv[:, dt, 3:4].opt(),
            start=(dt == 0), stop=(dt == ND - 1)))
        K.emit("tensor", lambda e, dt=dt, bkB=bkB: e.matmul(
            PB[bkB][0:1, 0:1], onec[:, 0:1], hl_t[:, dt:dt + 1],
            start=(dt == 0), stop=(dt == ND - 1)),
            cPE if dt == ND - 1 else None, 1)
    K.wait("vector", cPE, cPE.n)
    emv(lambda e, bkA=bkA: e.tensor_single_scalar(
        out=rb8[0:1, 0:1], in_=PB[bkA][0:1, 0:1], scalar=1.0 / D,
        op=ALU.mult))
    emv(lambda e, bkB=bkB: e.tensor_single_scalar(
        out=rb8[0:1, 1:2], in_=PB[bkB][0:1, 0:1], scalar=1.0 / D,
        op=ALU.mult))
    K.bank_set(bkA, cDV)
    K.bank_set(bkB, cDV)
    emv(lambda e: e.tensor_tensor(
        out=rb8[0:1, 2:3], in0=rb8[0:1, 0:1], in1=rb8[0:1, 0:1],
        op=ALU.mult))
    emv(lambda e: e.tensor_tensor(
        out=rb8[0:1, 2:3], in0=rb8[0:1, 1:2],
        in1=rb8[0:1, 2:3], op=ALU.subtract))
    K.wait("scalar", cDV)
    K.emit("scalar", lambda e: e.activation(
        out=rb8[0:1, 3:4], in_=rb8[0:1, 2:3], func=AF.Ln,
        bias=fc[EPS_LN][0:1, 0:1]), cAC, 1)
    K.emit("scalar", lambda e: e.activation(
        out=rb8[0:1, 3:4], in_=rb8[0:1, 3:4], func=AF.Exp,
        scale=-0.5), cAC, 1)
    K.wait("tensor", cAC, cAC.n)
    K.wait("tensor", cDV, cDV.n)
    bkA = next_pb()
    mm_group([lambda e, bkA=bkA: e.matmul(
        PB[bkA][:, 0:1], oner[0:1, :], rb8[0:1, 0:1], start=True,
        stop=True)],
        [lambda e, bkA=bkA: e.tensor_copy(out=mcol[:, 0:1],
                                          in_=PB[bkA][:, 0:1])], bkA)
    bkB = next_pb()
    mm_group([lambda e, bkB=bkB: e.matmul(
        PB[bkB][:, 0:1], oner[0:1, :], rb8[0:1, 3:4], start=True,
        stop=True)],
        [lambda e, bkB=bkB: e.tensor_copy(out=rscol[:, 0:1],
                                          in_=PB[bkB][:, 0:1])], bkB)
    K.wait("vector", cDV, cDV.n)
    for dt in range(ND):
        emv(lambda e, dt=dt: e.tensor_tensor(
            out=hl_t[:, dt:dt + 1], in0=xcv[:, dt, 3:4].opt(),
            in1=mcol[:, 0:1], op=ALU.subtract))
        emv(lambda e, dt=dt: e.tensor_single_scalar(
            out=hl_t[:, dt:dt + 1], in_=hl_t[:, dt:dt + 1],
            scalar=rscol[:, 0:1], op=ALU.mult))
        emv(lambda e, dt=dt: e.tensor_single_scalar(
            out=hl_t[:, dt:dt + 1], in_=hl_t[:, dt:dt + 1],
            scalar=smask[:, 0:1], op=ALU.mult))
    ch8 = K.chan("ch8")
    K.wait("sync", cDV, cDV.n)
    K.emit("sync", lambda e: e.dma_start(out=cc_h_in, in_=hl_t[:, :]),
           ch8, 16)
    K.wait("gpsimd", ch8, ch8.n)
    K.emit("gpsimd", lambda e: e.collective_compute(
        "AllReduce", ALU.add, replica_groups=[[0, 1], [2, 3], [4, 5], [6, 7]],
        ins=[cc_h_in], outs=[cc_h_out]), cCC, 1)
    K.wait("sync", cCC, cCC.n)
    K.emit("sync", lambda e: e.dma_start(out=hprev_u[:, :], in_=cc_h_out),
           ch8, 16)
    K.wait("vector", ch8, ch8.n)
    K.emit("vector", lambda e: e.tensor_single_scalar(
        out=hprev_u[:, :], in_=hprev_u[:, :], scalar=umask[:, 0:1],
        op=ALU.mult), cDV, 1)
    R3.reset(m38)

    ws8 = Stream(K, "p8w", 1)
    os8 = OStream(K, "p8r", 2)
    hcol_src = {0: hprev_u, 1: hkeep}
    wbb_guard = {"v": 0}
    for pan in range(2):
        t0p = pan * PAN
        hpt = [R2[dt][:, 0:PAN] for dt in range(ND)]
        xmt = [R2[dt][:, PAN:2 * PAN] for dt in range(ND)]
        ln_pass(xres2, t0p, PAN, hpt, False, out_guard=(cPE, cPE.n))
        m3pan = R3.mark()
        kp_t, kp_W = R3.carve_block(PAN, BF16, ND)
        wbf8t, wbf8W = R3.carve_block(128, F32, ND)
        wbb8t, wbb8W = R3.carve_block(128, BF16, ND)
        mt8 = R3.carve(PAN, F32)
        vtmp8 = R3.carve(512, F32)
        rst8 = R3.carve(PAN, BF16, 2)
        if pan == 0:
            K.wait("vector", cDV, cDV.n)
            for dt in range(ND):
                K.emit("vector", lambda e, dt=dt, hpt=hpt: e.tensor_copy(
                    out=hkeep[:, dt:dt + 1],
                    in_=hpt[dt][:, PAN - 1:PAN]), cDV, 1)
        hsrc = hcol_src[pan]
        kb_done = {}
        for wnm, mnm in (("r", "r"), ("kb", "k"), ("vb", "v")):
            if wnm == "vb":
                K.wait("vector", cAC, cAC.n)
            K.wait("vector", cPE, cPE.n)
            for dt in range(ND):
                emv(lambda e, dt=dt, mnm=mnm, hpt=hpt:
                    e.tensor_single_scalar(
                        out=mt8[:, :], in_=hpt[dt][:, :],
                        scalar=mixs[mnm][:, dt:dt + 1], op=ALU.mult))
                emv(lambda e, dt=dt, mnm=mnm, hpt=hpt, xmt=xmt:
                    e.scalar_tensor_tensor(
                        out=xmt[dt][:, 1:PAN], in0=hpt[dt][:, 0:PAN - 1],
                        scalar=imixs[mnm][:, dt:dt + 1],
                        in1=mt8[:, 1:PAN], op0=ALU.mult, op1=ALU.add))
                emv(lambda e, dt=dt, mnm=mnm, hsrc=hsrc,
                    xmt=xmt: e.scalar_tensor_tensor(
                        out=xmt[dt][:, 0:1], in0=hsrc[:, dt:dt + 1],
                        scalar=imixs[mnm][:, dt:dt + 1],
                        in1=mt8[:, 0:1], op0=ALU.mult, op1=ALU.add))
            K.wait("tensor", cDV, cDV.n)
            K.wait("sync", cPE, cPE.n)
            K.wait("sync", cDV, cDV.n)
            for ot in range(ND):
                iw = ws8.issue(lambda e, wnm=wnm, ot=ot: e.dma_start(
                    out=wbf8W, in_=fmv(wT[wnm])[:, :,
                                                ot * 128:(ot + 1) * 128]))
                ws8.wait("vector", iw)
                K.wait("vector", cPE, wbb_guard["v"])
                K.emit("vector", lambda e: e.tensor_copy(
                    out=wbb8W, in_=wbf8W), cDV, 1)
                ws8.set_done(iw, cDV)
                K.wait("tensor", cDV, cDV.n)
                for tc in range(2):
                    bk = next_pb()
                    mms = [(lambda e, bk=bk, dt=dt, tc=tc, xmt=xmt: e.matmul(
                        PB[bk][:, 0:512], wbb8t[dt],
                        xmt[dt][:, tc * 512:(tc + 1) * 512],
                        start=(dt == 0), stop=(dt == ND - 1)))
                        for dt in range(ND)]
                    if wnm == "r":
                        st = rst8[ot % 2]
                        if tc == 0:
                            os8.wait_free("scalar")
                        evs = [(lambda e, bk=bk, st=st, tc=tc: e.activation(
                            out=st[:, tc * 512:(tc + 1) * 512],
                            in_=PB[bk][:, 0:512], func=AF.Sigmoid))]
                        mm_group(mms, evs, bk, ev="scalar")
                    elif wnm == "kb":
                        evs = [(lambda e, bk=bk, ot=ot, tc=tc, kp_t=kp_t:
                                e.activation(
                                    out=kp_t[ot][:, tc * 512:(tc + 1) * 512],
                                    in_=PB[bk][:, 0:512], func=AF.Copy))]
                        mm_group(mms, evs, bk, ev="scalar")
                    else:
                        evs = [
                            (lambda e, bk=bk: e.tensor_copy(
                                out=vtmp8[:, 0:512], in_=PB[bk][:, 0:512])),
                            (lambda e, ot=ot, tc=tc, t0p=t0p,
                             kp_t=kp_t: e.tensor_tensor(
                                 out=R1bf[ot][:, t0p + tc * 512:
                                              t0p + (tc + 1) * 512],
                                 in0=kp_t[ot][:, tc * 512:(tc + 1) * 512],
                                 in1=vtmp8[:, 0:512], op=ALU.mult)),
                        ]
                        mm_group(mms, evs, bk)
                wbb_guard["v"] = cPE.n
                if wnm == "r":
                    os8.store(lambda e, ot=ot, t0p=t0p, st=rst8[ot % 2]:
                              e.dma_start(
                                  out=r_hbm[ot * 128:(ot + 1) * 128,
                                            t0p:t0p + PAN],
                                  in_=st[:, :]), cAC)
        R3.reset(m3pan)

    if upto == "b_proj":
        cdk = K.chan("cdk0")
        K.wait("sync", cDV, cDV.n)
        K.wait("sync", cPE, cPE.n)
        K.emit("sync", lambda e: e.dma_start(
            out=fmv(dbg_kv), in_=R1W.bitcast(BF16)), cdk, 16)
        return K
    K.fence((cPE, cDV, cAC, cGP), "p8s")
    m3s = R3.mark()
    wbt8 = R3.carve(T, F32)
    sc8 = R3.carve(T, F32)
    t28 = R3.carve(T, F32)
    kvf8 = R3.carve(T, F32)
    rsl8 = R3.carve(T, BF16, 2)
    for dt in range(ND):
        emv(lambda e: e.memset(wbt8[:, :], 1.0))
        emv(lambda e, dt=dt: e.tensor_single_scalar(
            out=wbt8[:, :], in_=wbt8[:, :], scalar=w_cols[:, dt:dt + 1],
            op=ALU.mult))
        emv(lambda e, dt=dt: e.tensor_copy(out=kvf8[:, :],
                                           in_=R1bf[dt][:, :]))
        emv(lambda e, dt=dt: e.tensor_tensor_scan(
            sc8[:, :], wbt8[:, :], kvf8[:, :], 0.0, ALU.mult, ALU.add))
        emv(lambda e, dt=dt: e.tensor_single_scalar(
            out=s_send[:, dt:dt + 1], in_=sc8[:, T - 1:T],
            scalar=smask[:, 0:1], op=ALU.mult))
    cs8 = K.chan("cs8")
    K.wait("sync", cDV, cDV.n)
    K.emit("sync", lambda e: e.dma_start(out=cc_s_in, in_=s_send[:, :]),
           cs8, 16)
    K.wait("gpsimd", cs8, cs8.n)
    K.emit("gpsimd", lambda e: e.collective_compute(
        "AllReduce", ALU.add, replica_groups=[[0, 1], [2, 3], [4, 5], [6, 7]],
        ins=[cc_s_in], outs=[cc_s_out]), cCC, 1)
    K.wait("sync", cCC, cCC.n)
    K.emit("sync", lambda e: e.dma_start(out=s_use[:, :], in_=cc_s_out),
           cs8, 16)
    K.wait("vector", cs8, cs8.n)
    K.emit("vector", lambda e: e.tensor_single_scalar(
        out=s_use[:, :], in_=s_use[:, :], scalar=umask[:, 0:1],
        op=ALU.mult), cDV, 1)

    if "dbg_kv" in dump:
        cdk = K.chan("cdk")
        K.wait("sync", cDV, cDV.n)
        K.emit("sync", lambda e: e.dma_start(
            out=fmv(dbg_kv), in_=R1W.bitcast(BF16)), cdk, 16)
    if upto == "b_scan1":
        return K
    rs8 = Stream(K, "p8rr", 2)
    for dt in range(ND):
        ir = rs8.issue(lambda e, dt=dt: e.dma_start(
            out=rsl8[dt % 2][:, :], in_=r_hbm[dt * 128:(dt + 1) * 128, :]))
        emv(lambda e: e.memset(wbt8[:, :], 1.0))
        emv(lambda e, dt=dt: e.tensor_single_scalar(
            out=wbt8[:, :], in_=wbt8[:, :], scalar=w_cols[:, dt:dt + 1],
            op=ALU.mult))
        emv(lambda e, dt=dt: e.tensor_copy(out=kvf8[:, :],
                                           in_=R1bf[dt][:, :]))
        emv(lambda e, dt=dt: e.tensor_tensor_scan(
            sc8[:, :], wbt8[:, :], kvf8[:, :], s_use[:, dt:dt + 1],
            ALU.mult, ALU.add))
        emv(lambda e, dt=dt: e.scalar_tensor_tensor(
            out=t28[:, 1:T], in0=R1bf[dt][:, 1:T],
            scalar=u_cols[:, dt:dt + 1], in1=sc8[:, 0:T - 1],
            op0=ALU.mult, op1=ALU.add))
        emv(lambda e, dt=dt: e.scalar_tensor_tensor(
            out=t28[:, 0:1], in0=R1bf[dt][:, 0:1],
            scalar=u_cols[:, dt:dt + 1], in1=s_use[:, dt:dt + 1],
            op0=ALU.mult, op1=ALU.add))
        rs8.wait("vector", ir)
        emv(lambda e, dt=dt: e.tensor_tensor(
            out=R1bf[dt][:, :], in0=t28[:, :], in1=rsl8[dt % 2][:, :],
            op=ALU.mult))
        rs8.set_done(ir, cDV)
    if "dbg_wkv" in dump:
        cdw = K.chan("cdw")
        K.wait("sync", cDV, cDV.n)
        K.emit("sync", lambda e: e.dma_start(
            out=fmv(dbg_wkv), in_=R1W.bitcast(BF16)), cdw, 16)
    R3.reset(m3s)

    K.fence((cPE, cDV, cAC, cGP), "p8o")
    m3o = R3.mark()
    wo8 = Stream(K, "p8wo", 2)
    x28 = Stream(K, "p8x2", 2)
    xo8 = OStream(K, "p8x3", 2)
    wof_t, wof_W = [], []
    wob_t, wob_W = [], []
    for i in range(2):
        t_, w_ = R3.carve_block(128, F32, ND)
        wof_t.append(t_)
        wof_W.append(w_)
        t_, w_ = R3.carve_block(128, BF16, ND)
        wob_t.append(t_)
        wob_W.append(w_)
    x2sl = R3.carve(T, F32, 2)
    x3st = R3.carve(T, F32, 2)
    cc8 = K.chan("cc8b")
    bfree8 = {}
    for ht in range(ND):
        sl = ht % 2
        iw = wo8.issue(lambda e, sl=sl, ht=ht: e.dma_start(
            out=wof_W[sl], in_=fmv(wT["o"])[:, :, ht * 128:(ht + 1) * 128]))
        ix = x28.issue(lambda e, sl=sl, ht=ht: e.dma_start(
            out=x2sl[sl][:, :], in_=fmv(xres2)[:, ht, :]))
        wo8.wait("vector", iw)
        if sl in bfree8:
            K.wait("vector", cPE, bfree8[sl])
        K.emit("vector", lambda e, sl=sl: e.tensor_copy(
            out=wob_W[sl], in_=wof_W[sl]), cc8, 1)
        wo8.set_done(iw, cDV)
        x28.wait("vector", ix)
        st = x3st[ht % 2]
        xo8.wait_free("vector")
        K.wait("tensor", cc8, cc8.n)
        if ht == 0:
            K.wait("tensor", cDV, cDV.n)
        for c in range(4):
            bk = next_pb()
            mms = [(lambda e, bk=bk, sl=sl, ot=ot, c=c: e.matmul(
                PB[bk][:, 0:512], wob_t[sl][ot],
                R1bf[ot][:, c * 512:(c + 1) * 512],
                start=(ot == 0), stop=(ot == ND - 1)))
                for ot in range(ND)]
            evs = [(lambda e, bk=bk, sl=sl, c=c, st=st: e.tensor_tensor(
                out=st[:, c * 512:(c + 1) * 512], in0=PB[bk][:, 0:512],
                in1=x2sl[sl][:, c * 512:(c + 1) * 512], op=ALU.add))]
            mm_group(mms, evs, bk)
        bfree8[sl] = cPE.n
        x28.set_done(ix, cDV)
        xo8.store(lambda e, st=st, ht=ht: e.dma_start(
            out=xres3[ht * 128:(ht + 1) * 128, :], in_=st[:, :]), cDV)
    R3.reset(m3o)
    if upto == "x3":
        return K

    ffn(xres3, out_fm, "f2")
    return K


def _shared_inputs(inputs):
    def tp(a):
        return np.ascontiguousarray(np.asarray(a).T)
    sh = {}
    sh["wqT"] = tp(inputs["L0_attn_a_Wq"])
    sh["wkT"] = tp(inputs["L0_attn_a_Wk"])
    sh["wvT"] = tp(inputs["L0_attn_a_Wv"])
    sh["wgT"] = tp(inputs["L0_attn_a_Wg"])
    sh["wpT"] = tp(inputs["L0_attn_a_Wp"])
    sh["wrT"] = tp(inputs["L0_attn_b_Wr"])
    sh["wkbT"] = tp(inputs["L0_attn_b_Wk"])
    sh["wvbT"] = tp(inputs["L0_attn_b_Wv"])
    sh["woT"] = tp(inputs["L0_attn_b_Wo"])
    sh["td_hp"] = np.ascontiguousarray(
        np.asarray(inputs["L0_attn_b_time_decay"], dtype=np.float32))
    sh["tf_hp"] = np.ascontiguousarray(
        np.asarray(inputs["L0_attn_b_time_first"], dtype=np.float32))
    sh["mixk_hp"] = np.asarray(
        inputs["L0_attn_b_time_mix_k"], dtype=np.float32).reshape(16, 128)
    sh["mixv_hp"] = np.asarray(
        inputs["L0_attn_b_time_mix_v"], dtype=np.float32).reshape(16, 128)
    sh["mixr_hp"] = np.asarray(
        inputs["L0_attn_b_time_mix_r"], dtype=np.float32).reshape(16, 128)
    sh["gwT"] = tp(inputs["L0_ffn_gate_w"])
    sh["gb_row"] = np.asarray(
        inputs["L0_ffn_gate_b"], dtype=np.float32).reshape(1, 8)
    bf = ml_dtypes.bfloat16
    w1 = np.asarray(inputs["L0_ffn_w1"])
    w3 = np.asarray(inputs["L0_ffn_w3"])
    w2 = np.asarray(inputs["L0_ffn_w2"])
    sw1 = np.asarray(inputs["L0_ffn_sw1"])
    sw3 = np.asarray(inputs["L0_ffn_sw3"])
    sw2 = np.asarray(inputs["L0_ffn_sw2"])
    sh["w1T_all"] = np.ascontiguousarray(np.concatenate(
        [w1.transpose(0, 2, 1), sw1.T[None]], axis=0)).astype(bf)
    sh["w3T_all"] = np.ascontiguousarray(np.concatenate(
        [w3.transpose(0, 2, 1), sw3.T[None]], axis=0)).astype(bf)
    sh["w2T_all"] = np.ascontiguousarray(np.concatenate(
        [w2.transpose(0, 2, 1), sw2.T[None]], axis=0)).astype(bf)
    sh["ident_f"] = np.eye(128, dtype=np.float32)
    sh["triu_m"] = np.triu(np.ones((128, 128), dtype=np.float32))
    sh["ones_col"] = np.ones((128, 1), dtype=np.float32)
    sh["ones_row"] = np.ones((1, 128), dtype=np.float32)
    selm = np.zeros((8, 8 * 128), dtype=np.float32)
    for e in range(8):
        selm[e, e * 128:(e + 1) * 128] = 1.0
    sh["sel_in"] = selm
    return sh


def make_in_maps(inputs, dump=()):
    sh = _shared_inputs(inputs)
    x = np.asarray(inputs["L0_attn_a_x"])
    in_maps = []
    for c in range(8):
        b, hf = c // 2, c % 2
        m = dict(sh)
        m["x0"] = np.ascontiguousarray(
            x[b, hf * T:(hf + 1) * T, :].T).astype(np.float32)
        m["umask_in"] = np.full((128, 1), float(hf), dtype=np.float32)
        m["smask_in"] = np.full((128, 1), float(1 - hf), dtype=np.float32)
        in_maps.append(m)
    return in_maps


def kernel(**inputs):
    K = build_nc()
    nc = K.nc
    with nc.Block() as block:
        K.finalize(block)
    in_maps = make_in_maps(inputs)
    res = run_bass_kernel_spmd(nc, in_maps, core_ids=list(range(8)))
    out = np.empty((4, 4096, D), dtype=np.float32)
    for c in range(8):
        b, hf = c // 2, c % 2
        out[b, hf * T:(hf + 1) * T, :] = res.results[c]["out_fm"].T
    return out


# version 04
# speedup vs baseline: 1.0273x; 1.0273x over previous
import numpy as np
import ml_dtypes
import concourse.bass as bass
import concourse.mybir as mybir
from concourse.bass_utils import run_bass_kernel_spmd

F32 = mybir.dt.float32
BF16 = mybir.dt.bfloat16
AF = mybir.ActivationFunctionType
ALU = mybir.AluOpType
AXX = mybir.AxisListType.X

D = 2048
T = 2048
ND = 16
PAN = 1024
ACH = 256
NACH = T // ACH
INTER = 1408
NEXP = 9
EPS_LN = 1e-5
EPS_RMS = float(np.finfo(np.float32).eps)

ENGINES = ("sync", "vector", "scalar", "tensor", "gpsimd")
_uid = [0]


def uname(p):
    _uid[0] += 1
    return f"{p}{_uid[0]}"


class Chan:
    __slots__ = ("sem", "n")

    def __init__(self, sem):
        self.sem = sem
        self.n = 0


class KB:
    def __init__(self):
        self.nc = bass.Bass("TRN2", target_bir_lowering=False)
        self.ops = {e: [] for e in ENGINES}
        self.chans = []
        self.bankfree = {}
        self.pending = []

    def chan(self, name):
        c = Chan(self.nc.alloc_semaphore(name=uname(name)))
        self.chans.append(c)
        return c

    def emit(self, eng, fn, chan=None, inc=1):
        if chan is not None:
            chan.n += inc
            self.ops[eng].append(
                lambda e, f=fn, c=chan, i=inc: f(e).then_inc(c.sem, i))
        else:
            self.ops[eng].append(fn)

    def wait(self, eng, chan, val=None):
        v = chan.n if val is None else val
        if v > 0:
            self.ops[eng].append(lambda e, c=chan, vv=v: e.wait_ge(c.sem, vv))

    def bank_wait(self, key):
        if key in self.bankfree:
            c, v = self.bankfree[key]
            self.wait("tensor", c, v)

    def bank_set(self, key, chan):
        self.bankfree[key] = (chan, chan.n)

    def note_pending(self, chan):
        self.pending.append((chan, chan.n))

    def fence(self, main_chans, tag=""):
        pend = {}
        for c, v in self.pending:
            pend[id(c)] = (c, max(v, pend.get(id(c), (c, 0))[1]))
        self.pending = []
        targets = [(c, c.n) for c in main_chans if c.n > 0]
        targets += list(pend.values())
        for eng in ENGINES:
            for c, v in targets:
                self.wait(eng, c, v)

    def finalize(self, block):
        def runner(name):
            def body(e):
                for fn in self.ops[name]:
                    fn(e)
                if name == "sync":
                    for c in self.chans:
                        if c.n > 0:
                            e.wait_ge(c.sem, c.n)
            return body
        block.sync(runner("sync"))
        block.vector(runner("vector"))
        block.scalar(runner("scalar"))
        block.tensor(runner("tensor"))
        block.gpsimd(runner("gpsimd"))


class Arena:
    def __init__(self, nc, name, kib):
        self.h = nc.alloc_sbuf_tensor(name, [128, kib * 1024], mybir.dt.uint8)
        self.base = self.h.ap()
        self.size = kib * 1024
        self.off = 0

    def carve(self, cols, dtype, n=None):
        esz = {F32: 4, BF16: 2}[dtype]
        out = []
        for _ in range(n or 1):
            b = (self.off + 63) & ~63
            nb = cols * esz
            assert b + nb <= self.size, f"arena overflow {b + nb}>{self.size}"
            out.append(self.base[:, b:b + nb].bitcast(dtype))
            self.off = b + nb
        return out if n else out[0]

    def carve_block(self, cols, dtype, n):
        esz = {F32: 4, BF16: 2}[dtype]
        b = (self.off + 63) & ~63
        nb = cols * esz
        assert b + n * nb <= self.size, "arena overflow (block)"
        whole = self.base[:, b:b + n * nb].bitcast(dtype)
        tiles = [self.base[:, b + i * nb:b + (i + 1) * nb].bitcast(dtype)
                 for i in range(n)]
        self.off = b + n * nb
        return tiles, whole.rearrange("p (n c) -> p n c", n=n)

    def mark(self):
        return self.off

    def reset(self, mark=0):
        self.off = mark


class Stream:
    def __init__(self, K, name, nslots, eng="sync"):
        self.K = K
        self.n = nslots
        self.eng = eng
        self.lch = [K.chan(f"{name}l") for _ in range(nslots)]
        self.i = 0
        self.loaded = []
        self.done = {}

    def issue(self, emits):
        K = self.K
        s = self.i % self.n
        prev = self.i - self.n
        if prev >= 0:
            c, v = self.done[prev]
            K.wait(self.eng, c, v)
        ch = self.lch[s]
        if not isinstance(emits, list):
            emits = [emits]
        for f in emits:
            K.emit(self.eng, f, ch, 16)
        self.loaded.append((s, ch.n))
        self.i += 1
        return self.i - 1

    def wait(self, eng, i):
        s, v = self.loaded[i]
        self.K.wait(eng, self.lch[s], v)

    def set_done(self, i, chan):
        self.done[i] = (chan, chan.n)


class OStream:
    def __init__(self, K, name, nslots):
        self.K = K
        self.n = nslots
        self.sch = [K.chan(f"{name}s") for _ in range(nslots)]
        self.i = 0

    def wait_free(self, eng):
        s = self.i % self.n
        self.K.wait(eng, self.sch[s], self.sch[s].n)

    def store(self, emit, prod_chan):
        K = self.K
        s = self.i % self.n
        K.wait("sync", prod_chan, prod_chan.n)
        K.emit("sync", emit, self.sch[s], 16)
        K.note_pending(self.sch[s])
        self.i += 1


def fmv(dram_ap, p=128):
    return dram_ap.rearrange("(dt p) t -> p dt t", p=p)


def build_nc(dump=(), upto=None):
    K = KB()
    nc = K.nc
    cPE = K.chan("cPE")
    cDV = K.chan("cDV")
    cAC = K.chan("cAC")
    cGP = K.chan("cGP")
    cCC = K.chan("cCC")

    def dram_in(name, shape, dtype):
        kind = "ExternalOutput" if name in dump else "ExternalInput"
        return nc.dram_tensor(name, shape, dtype, kind=kind).ap()

    def scratch(name, shape, dtype):
        kind = "ExternalOutput" if name in dump else "Internal"
        return nc.dram_tensor(name, shape, dtype, kind=kind).ap()

    x0 = dram_in("x0", [D, T], F32)
    wT = {nm: dram_in("w" + nm + "T", [D, D], F32)
          for nm in ("q", "k", "v", "g", "p", "r", "kb", "vb", "o")}
    td_hp = dram_in("td_hp", [16, 128], F32)
    tf_hp = dram_in("tf_hp", [16, 128], F32)
    mix_hp = {nm: dram_in("mix" + nm + "_hp", [16, 128], F32)
              for nm in ("k", "v", "r")}
    gwT_d = dram_in("gwT", [D, 8], F32)
    gb_d = dram_in("gb_row", [1, 8], F32)
    w1T_all = dram_in("w1T_all", [NEXP, D, INTER], BF16)
    w3T_all = dram_in("w3T_all", [NEXP, D, INTER], BF16)
    w2T_all = dram_in("w2T_all", [NEXP, INTER, D], BF16)
    ident_d = dram_in("ident_f", [128, 128], F32)
    triu_d = dram_in("triu_m", [128, 128], F32)
    onec_d = dram_in("ones_col", [128, 1], F32)
    oner_d = dram_in("ones_row", [1, 128], F32)
    sel_d = dram_in("sel_in", [8, 8 * 128], F32)
    umask_d = dram_in("umask_in", [128, 1], F32)
    smask_d = dram_in("smask_in", [128, 1], F32)

    out_fm = nc.dram_tensor("out_fm", [D, T], F32, kind="ExternalOutput").ap()

    q_fm = scratch("q_fm", [D, T], BF16)
    k_fm = scratch("k_fm", [D, T], BF16)
    v_fm = scratch("v_fm", [D, T], BF16)
    g_fm = scratch("g_fm", [D, T], BF16)
    o_loc = scratch("o_loc", [D, T], BF16)
    og_hbm = scratch("og_hbm", [D, T], BF16)
    r_hbm = scratch("r_hbm", [D, T], BF16)
    xres1 = scratch("xres1", [D, T], F32)
    xres2 = scratch("xres2", [D, T], F32)
    xres3 = scratch("xres3", [D, T], F32)
    dbg_cw = scratch("dbg_cw", [8, T], F32)
    dbg_gs = scratch("dbg_gs", [128, 128], F32)
    dbg_cols = scratch("dbg_cols", [128, 96], F32)
    dbg_kv = scratch("dbg_kv", [D, T], BF16)
    dbg_wkv = scratch("dbg_wkv", [D, T], BF16)
    cc_kv_in = [nc.dram_tensor(f"cc_kv_in{i}", [D, 1024], BF16).ap()
                for i in range(2)]
    cc_kv_out = [nc.dram_tensor(f"cc_kv_out{i}", [D, 1024], BF16).ap()
                 for i in range(2)]
    cc_h_in = nc.dram_tensor("cc_h_in", [128, 16], F32).ap()
    cc_h_out = nc.dram_tensor("cc_h_out", [128, 16], F32).ap()
    cc_s_in = nc.dram_tensor("cc_s_in", [128, 16], F32).ap()
    cc_s_out = nc.dram_tensor("cc_s_out", [128, 16], F32).ap()

    cst = Arena(nc, "cst", 16)
    ident = cst.carve(128, F32)
    triu = cst.carve(128, F32)
    onec = cst.carve(1, F32)
    oner = cst.carve(128, F32)
    sel = cst.carve(8 * 128, F32)
    umask = cst.carve(1, F32)
    smask = cst.carve(1, F32)
    gbb = cst.carve(8, F32)
    gbrow_t = cst.carve(8, F32)
    hp_ld = cst.carve(128, F32)
    mixs = {nm: cst.carve(16, F32) for nm in ("k", "v", "r")}
    imixs = {nm: cst.carve(16, F32) for nm in ("k", "v", "r")}
    tds = cst.carve(16, F32)
    tfs = cst.carve(16, F32)
    u_cols = cst.carve(16, F32)
    w_cols = cst.carve(16, F32)
    hprev_u = cst.carve(16, F32)
    s_use = cst.carve(16, F32)
    s_send = cst.carve(16, F32)
    hkeep = cst.carve(16, F32)
    gwTs, gwTsW = cst.carve_block(8, F32, ND)
    mcol = cst.carve(1, F32)
    m2col = cst.carve(1, F32)
    rscol = cst.carve(1, F32)

    R1a = Arena(nc, "R1", 64)
    R1, R1W = R1a.carve_block(1024, F32, ND)
    R1bf = [r.bitcast(BF16) for r in R1]
    R2a = Arena(nc, "R2", 64)
    R2, R2W = R2a.carve_block(2048, BF16, ND)
    R3 = Arena(nc, "R3", 56)

    PBH = [nc.alloc_psum_tensor(f"pb{i}", [128, 512], F32) for i in range(6)]
    PB = [t.ap() for t in PBH]
    PWH = nc.alloc_psum_tensor("pw", [128, 1024], F32)
    PW = PWH.ap()

    pbrot = {"i": 0}

    def next_pb():
        i = pbrot["i"]
        pbrot["i"] = (i + 1) % 6
        return i

    def emv(fn):
        K.wait("vector", cDV, cDV.n)
        K.emit("vector", fn, cDV, 1)

    def mm_group(mms, evacs, key, ev="vector"):
        keys = key if isinstance(key, (list, tuple)) else [key]
        for kk in keys:
            K.bank_wait(kk)
        for m in mms[:-1]:
            K.emit("tensor", m)
        K.emit("tensor", mms[-1], cPE, 1)
        ch = cDV if ev == "vector" else cAC
        K.wait(ev, cPE, cPE.n)
        for f in evacs[:-1]:
            K.emit(ev, f)
        K.emit(ev, evacs[-1], ch, 1)
        for kk in keys:
            K.bank_set(kk, ch)

    c0 = K.chan("c0")
    for dst, src in [
        (ident[:, :], ident_d), (triu[:, :], triu_d),
        (onec[:, 0:1], onec_d), (oner[0:1, :], oner_d),
        (sel[0:8, :], sel_d), (umask[:, 0:1], umask_d),
        (smask[:, 0:1], smask_d), (gbrow_t[0:1, :], gb_d),
        (gwTsW, gwT_d.rearrange("(dt p) c -> p dt c", p=128)),
    ]:
        K.emit("sync", lambda e, d=dst, s=src: e.dma_start(out=d, in_=s),
               c0, 16)
    hp_list = [(td_hp, tds), (tf_hp, tfs)] + \
        [(mix_hp[nm], mixs[nm]) for nm in ("k", "v", "r")]
    K.wait("tensor", c0)
    for src, dst in hp_list:
        K.wait("sync", cPE, cPE.n)
        K.emit("sync", lambda e, s=src: e.dma_start(out=hp_ld[0:16, :],
                                                    in_=s), c0, 16)
        K.wait("tensor", c0, c0.n)
        bk = next_pb()
        mm_group(
            [lambda e, bk=bk: e.transpose(PB[bk][0:128, 0:16],
                                          hp_ld[0:16, :],
                                          ident[0:16, 0:16])],
            [lambda e, bk=bk, d=dst: e.tensor_copy(out=d[:, :],
                                                   in_=PB[bk][0:128, 0:16])],
            bk)
    K.wait("scalar", cDV)
    K.emit("scalar", lambda e: e.activation(out=u_cols[:, :], in_=tds[:, :],
                                            func=AF.Exp), cAC, 1)
    K.emit("scalar", lambda e: e.activation(out=w_cols[:, :], in_=u_cols[:, :],
                                            func=AF.Exp, scale=-1.0), cAC, 1)
    K.wait("vector", c0)
    for nm in ("k", "v", "r"):
        K.emit("vector", lambda e, nm=nm: e.tensor_scalar(
            out=imixs[nm][:, :], in0=mixs[nm][:, :], scalar1=-1.0,
            scalar2=1.0, op0=ALU.mult, op1=ALU.add), cDV, 1)
    bk = next_pb()
    mm_group([lambda e, bk=bk: e.matmul(PB[bk][:, 0:8], oner[0:1, :],
                                        gbrow_t[0:1, :], start=True,
                                        stop=True)],
             [lambda e, bk=bk: e.tensor_copy(out=gbb[:, :],
                                             in_=PB[bk][:, 0:8])], bk)
    fc = {}
    for v in (1.0, EPS_LN, EPS_LN * EPS_RMS, -1e30):
        col = cst.carve(1, F32)
        K.emit("vector", lambda e, col=col, v=v: e.memset(col[:, 0:1], v),
               cDV, 1)
        fc[v] = col
    if "dbg_cols" in dump:
        cdc = K.chan("cdc")
        K.wait("sync", cDV, cDV.n)
        K.wait("sync", cAC, cAC.n)
        for i, tsrc in enumerate([tds, tfs, u_cols, w_cols, mixs["k"],
                                  imixs["k"]]):
            K.emit("sync", lambda e, i=i, t=tsrc: e.dma_start(
                out=dbg_cols[:, i * 16:(i + 1) * 16], in_=t[:, :]), cdc, 16)

    def ln_pass(src_dram, t0, width, out_bf, fused_rms, keep_x=None,
                out_guard=None, gate=False):
        nch = width // 512
        nxs = 1 if width == T else 2
        for eng in ("sync", "vector", "scalar", "tensor", "gpsimd"):
            for ch_ in (cPE, cDV, cAC):
                K.wait(eng, ch_, ch_.n)
        m3 = R3.mark()
        xsl = None if keep_x is not None else R3.carve(width, F32, nxs)
        sqsl = R3.carve(width, F32, nxs)
        mb = R3.carve(width, F32)
        rb = R3.carve(width, F32)
        tmsl = R3.carve(width, F32)
        xnfsl = R3.carve(width, F32, 2) if gate else None
        gate_sc = R3.carve((width // 128) * 8, F32) if gate else None
        regs = [PB[i][0:1, 0:512] for i in range(6)] + \
               [PW[0:1, 0:512], PW[0:1, 512:1024]]
        s1r = regs[:nch]
        s2r = regs[nch:2 * nch]
        bank_keys = list(range(2 * nch)) if nch <= 3 else [0, 1, 2, 3, 4, 5,
                                                           "pw"]
        sq_free = {}
        cx = K.chan("lncx")
        xst = Stream(K, "lnx", nxs) if keep_x is None else None
        for dt in range(ND):
            if keep_x is None:
                it = xst.issue(lambda e, dt=dt: e.dma_start(
                    out=xsl[dt % nxs][:, 0:width],
                    in_=fmv(src_dram)[:, dt, t0:t0 + width]))
                xst.wait("scalar", it)
                xd = xsl[dt % nxs]
            else:
                xd = keep_x[dt]
                K.emit("sync", lambda e, dt=dt, xd=xd: e.dma_start(
                    out=xd[:, 0:width],
                    in_=fmv(src_dram)[:, dt, t0:t0 + width]), cx, 16)
                K.wait("scalar", cx, cx.n)
            sq = sqsl[dt % nxs]
            if dt >= nxs:
                K.wait("scalar", cPE, sq_free[dt - nxs])
            K.emit("scalar", lambda e, sq=sq, xd=xd: e.activation(
                out=sq[:, 0:width], in_=xd[:, 0:width], func=AF.Square),
                cAC, 1)
            if dt == 0:
                for bkk in set(bank_keys):
                    K.bank_wait(bkk)
            if keep_x is None:
                xst.wait("tensor", dt)
            else:
                K.wait("tensor", cx, cx.n)
            K.wait("tensor", cAC, cAC.n)
            for c in range(nch):
                K.emit("tensor", lambda e, r=s1r[c], xd=xd, c=c, dt=dt:
                       e.matmul(r, onec[:, 0:1],
                                xd[:, c * 512:(c + 1) * 512],
                                start=(dt == 0), stop=(dt == ND - 1)))
            for c in range(nch):
                K.emit("tensor", lambda e, r=s2r[c], sq=sq, c=c, dt=dt:
                       e.matmul(r, onec[:, 0:1],
                                sq[:, c * 512:(c + 1) * 512],
                                start=(dt == 0), stop=(dt == ND - 1)),
                       cPE if c == nch - 1 else None, 1)
            sq_free[dt] = cPE.n
            if keep_x is None:
                xst.set_done(dt, cPE)
        rm = mb[0:1, 0:width]
        re2 = rb[0:1, 0:width]
        rv = tmsl[0:1, 0:width]
        K.wait("vector", cPE, cPE.n)
        for c in range(nch):
            K.emit("vector", lambda e, c=c, rm=rm: e.tensor_single_scalar(
                out=rm[:, c * 512:(c + 1) * 512], in_=s1r[c],
                scalar=1.0 / D, op=ALU.mult), cDV, 1)
            K.emit("vector", lambda e, c=c, re2=re2: e.tensor_single_scalar(
                out=re2[:, c * 512:(c + 1) * 512], in_=s2r[c],
                scalar=1.0 / D, op=ALU.mult), cDV, 1)
        for bkk in set(bank_keys):
            K.bank_set(bkk, cDV)
        K.emit("vector", lambda e: e.tensor_tensor(
            out=rv, in0=rm, in1=rm, op=ALU.mult), cDV, 1)
        K.emit("vector", lambda e: e.tensor_tensor(
            out=rv, in0=re2, in1=rv, op=ALU.subtract), cDV, 1)
        if fused_rms:
            K.emit("vector", lambda e: e.tensor_single_scalar(
                out=rv, in_=rv, scalar=1.0 + EPS_RMS, op=ALU.mult), cDV, 1)
        K.wait("scalar", cDV)
        bcol = fc[EPS_LN * EPS_RMS] if fused_rms else fc[EPS_LN]
        K.emit("scalar", lambda e, bcol=bcol: e.activation(
            out=re2, in_=rv, func=AF.Ln, bias=bcol[0:1, 0:1]), cAC, 1)
        K.emit("scalar", lambda e: e.activation(
            out=re2, in_=re2, func=AF.Exp, scale=-0.5), cAC, 1)
        K.wait("tensor", cAC)
        for c in range(nch):
            for rsrc, dstb in ((rm, mb), (re2, rb)):
                bk = next_pb()
                mm_group(
                    [lambda e, bk=bk, rsrc=rsrc, c=c: e.matmul(
                        PB[bk][:, 0:512], oner[0:1, :],
                        rsrc[:, c * 512:(c + 1) * 512], start=True,
                        stop=True)],
                    [lambda e, bk=bk, dstb=dstb, c=c: e.tensor_copy(
                        out=dstb[:, c * 512:(c + 1) * 512],
                        in_=PB[bk][:, 0:512])], bk)
        gk = None
        gate_free = {}
        if gate:
            gk = next_pb()
            K.bank_wait(gk)
        if out_guard is not None:
            K.wait("vector", out_guard[0], out_guard[1])
        xst2 = Stream(K, "lnx2", nxs) if keep_x is None else None
        if keep_x is None:
            K.wait("sync", cPE, cPE.n)
        for dt in range(ND):
            if keep_x is None:
                it = xst2.issue(lambda e, dt=dt: e.dma_start(
                    out=xsl[dt % nxs][:, 0:width],
                    in_=fmv(src_dram)[:, dt, t0:t0 + width]))
                xst2.wait("vector", it)
                xd = xsl[dt % nxs]
            else:
                xd = keep_x[dt]
            K.wait("vector", cDV, cDV.n)
            K.emit("vector", lambda e, xd=xd: e.tensor_tensor(
                out=tmsl[:, 0:width], in0=xd[:, 0:width],
                in1=mb[:, 0:width], op=ALU.subtract))
            if gate:
                xnf = xnfsl[dt % 2]
                if dt >= 2:
                    K.wait("vector", cPE, gate_free[dt - 2])
                K.emit("vector", lambda e, xnf=xnf: e.tensor_tensor(
                    out=xnf[:, 0:width], in0=tmsl[:, 0:width],
                    in1=rb[:, 0:width], op=ALU.mult), cDV, 1)
                K.emit("vector", lambda e, o=out_bf[dt], xnf=xnf:
                       e.tensor_copy(out=o[:, 0:width],
                                     in_=xnf[:, 0:width]), cDV, 1)
                K.wait("tensor", cDV, cDV.n)
                nsub = width // 128
                for ts_ in range(nsub):
                    K.emit("tensor", lambda e, xnf=xnf, ts_=ts_, dt=dt, gk=gk:
                           e.matmul(PB[gk][:, ts_ * 8:(ts_ + 1) * 8],
                                    xnf[:, ts_ * 128:(ts_ + 1) * 128],
                                    gwTs[dt][0:128, 0:8],
                                    start=(dt == 0 and ts_ == 0),
                                    stop=(dt == ND - 1)),
                           cPE if ts_ == nsub - 1 else None, 1)
                gate_free[dt] = cPE.n
            else:
                K.emit("vector", lambda e, o=out_bf[dt]: e.tensor_tensor(
                    out=o[:, 0:width], in0=tmsl[:, 0:width],
                    in1=rb[:, 0:width], op=ALU.mult), cDV, 1)
            if keep_x is None:
                xst2.set_done(dt, cDV)
        if gate:
            K.wait("vector", cPE, cPE.n)
            K.emit("vector", lambda e, gk=gk: e.tensor_copy(
                out=gate_sc[:, :], in_=PB[gk][:, 0:(width // 128) * 8]),
                cDV, 1)
            K.bank_set(gk, cDV)
            return gate_sc, m3
        R3.reset(m3)
        return None, m3

    m3base = R3.mark()
    wcf, wcfW = [], []
    for i in range(2):
        t_, w_ = R3.carve_block(128, F32, ND)
        wcf.append(t_)
        wcfW.append(w_)
    m3p2 = R3.mark()
    wst_p2 = Stream(K, "p2w", 2)
    ost_fm = OStream(K, "pfo", 2)
    ccast2 = K.chan("pfc")
    bfree_p2 = {}
    for ot in range(2):
        wst_p2.issue(lambda e, sl=ot % 2, ot=ot: e.dma_start(
            out=wcfW[sl], in_=fmv(wT["q"])[:, :, ot * 128:(ot + 1) * 128]))

    ln_pass(x0, 0, T, R2, True)

    K.fence((cPE, cDV, cAC, cGP), "p2")
    R3.reset(m3p2)
    wcb, wcbW = [], []
    for i in range(2):
        t_, w_ = R3.carve_block(128, BF16, ND)
        wcb.append(t_)
        wcbW.append(w_)
    stg2 = R3.carve(2048, BF16, 2)

    def proj_a(wdram, act, src_tiles, fm_out, npre=0):
        for ot in range(ND):
            sl = ot % 2
            if ot >= npre:
                it = wst_p2.issue(lambda e, sl=sl, w=wdram, ot=ot:
                                  e.dma_start(
                                      out=wcfW[sl],
                                      in_=fmv(w)[:, :,
                                                 ot * 128:(ot + 1) * 128]))
            else:
                it = ot
            wst_p2.wait("vector", it)
            if sl in bfree_p2:
                K.wait("vector", cPE, bfree_p2[sl])
            K.emit("vector", lambda e, sl=sl: e.tensor_copy(
                out=wcbW[sl], in_=wcfW[sl]), ccast2, 1)
            wst_p2.set_done(it, cDV)
            K.wait("tensor", ccast2, ccast2.n)
            st = stg2[ot % 2]
            ost_fm.wait_free("scalar")
            for c in range(4):
                bk = next_pb()
                mms = [(lambda e, bk=bk, sl=sl, dt=dt, c=c: e.matmul(
                    PB[bk][:, 0:512], wcb[sl][dt],
                    src_tiles[dt][:, c * 512:(c + 1) * 512],
                    start=(dt == 0), stop=(dt == ND - 1)))
                    for dt in range(ND)]
                evs = [(lambda e, bk=bk, st=st, c=c: e.activation(
                    out=st[:, c * 512:(c + 1) * 512],
                    in_=PB[bk][:, 0:512], func=act))]
                mm_group(mms, evs, bk, ev="scalar")
            ost_fm.store(lambda e, st=st, o=fm_out, ot=ot:
                         e.dma_start(out=o[ot * 128:(ot + 1) * 128, :],
                                     in_=st[:, :]), cAC)
            bfree_p2[sl] = cPE.n

    proj_a(wT["q"], AF.Silu, R2, q_fm, npre=2)
    proj_a(wT["k"], AF.Silu, R2, k_fm)
    proj_a(wT["v"], AF.Silu, R2, v_fm)
    proj_a(wT["g"], AF.Sigmoid, R2, g_fm)
    R3.reset(m3base)
    if upto == "proj_a":
        return K

    K.fence((cPE, cDV, cAC, cGP), "p3")
    m3p3 = R3.mark()
    qc_t, qc_W = [], []
    for i in range(2):
        t_, w_ = R3.carve_block(ACH, BF16, ND)
        qc_t.append(t_)
        qc_W.append(w_)
    kc_t, kc_W = R3.carve_block(ACH, BF16, ND)
    kt_t, kt_W = R3.carve_block(2048, BF16, 2)
    vt_t, vt_W = R3.carve_block(1024, BF16, 2)
    sT = []
    for cch in range(NACH):
        t_, _ = R3.carve_block(ACH, BF16, 2)
        sT.append(t_)
    ol_t0, ol_W0 = R3.carve_block(ACH, BF16, 8)
    ol_t = [ol_t0, ol_t0]
    ol_W = [ol_W0, ol_W0]
    snd = R3.carve(1024, BF16, 2)

    for cch in range(NACH):
        K.emit("gpsimd", lambda e, cch=cch: e.memset(
            sT[cch][1][:, 0:128], 0.0), cGP, 1)
    sT_ready = 0
    qs = Stream(K, "p3q", 2)
    ks = Stream(K, "p3k", 1)
    kts = Stream(K, "p3kt", 1)
    vs = Stream(K, "p3v", 1)
    oss = OStream(K, "p3o", 1)
    sndst = OStream(K, "p3snd", 2)
    for sp in range(2):
        K.wait("gpsimd", cPE, cPE.n)
        K.wait("gpsimd", cDV, cDV.n)
        K.wait("gpsimd", cAC, cAC.n)
        for dt in range(ND):
            K.emit("gpsimd", lambda e, dt=dt: e.memset(R1[dt][:, :], 0.0),
                   cGP, 1)
            K.emit("gpsimd", lambda e, dt=dt: e.memset(
                R2[dt][:, 0:1024], 0.0), cGP, 1)
        K.wait("sync", cPE, cPE.n)
        K.wait("sync", cDV, cDV.n)
        kvev = {}
        for cch in range(NACH):
            t0c = cch * ACH
            sl = cch % 2
            iq = qs.issue(lambda e, sl=sl, t0c=t0c: e.dma_start(
                out=qc_W[sl], in_=fmv(q_fm)[:, :, t0c:t0c + ACH]))
            if sp == 0:
                ik = ks.issue(lambda e, t0c=t0c: e.dma_start(
                    out=kc_W, in_=fmv(k_fm)[:, :, t0c:t0c + ACH]))
            ikt = kts.issue([
                (lambda e, cch=cch, jt=jt2: e.dma_start_transpose(
                    out=kt_t[jt][:, :],
                    in_=k_fm[:, cch * ACH + jt * 128:
                             cch * ACH + (jt + 1) * 128]))
                for jt2 in range(2) for jt in (jt2,)])
            iv = vs.issue([
                (lambda e, cch=cch, sp=sp, jt=jt2: e.dma_start_transpose(
                    out=vt_t[jt][:, :],
                    in_=v_fm[sp * 1024:(sp + 1) * 1024,
                             cch * ACH + jt * 128:
                             cch * ACH + (jt + 1) * 128]))
                for jt2 in range(2) for jt in (jt2,)])
            if cch == 0:
                K.wait("tensor", cGP, cGP.n)
            if sp == 0:
                ks.wait("tensor", ik)
                qs.wait("tensor", iq)
                for jt in range(2):
                    nn = ACH - jt * 128
                    bk = 4 + jt
                    mms = [(lambda e, bk=bk, dt=dt, jt=jt, nn=nn, sl=sl:
                            e.matmul(
                                PB[bk][:, 0:nn],
                                kc_t[dt][:, jt * 128:(jt + 1) * 128],
                                qc_t[sl][dt][:, jt * 128:jt * 128 + nn],
                                start=(dt == 0), stop=(dt == ND - 1)))
                           for dt in range(ND)]
                    evs = [(lambda e, bk=bk, cch=cch, jt=jt: e.tensor_tensor(
                        out=sT[cch][jt][:, jt * 128:jt * 128 + 128],
                        in0=PB[bk][:, 0:128], in1=triu[:, :], op=ALU.mult))]
                    if jt == 0:
                        evs.append((lambda e, bk=bk, cch=cch: e.tensor_copy(
                            out=sT[cch][0][:, 128:256],
                            in_=PB[bk][:, 128:256])))
                    mm_group(mms, evs, bk)
                ks.set_done(ik, cPE)
                K.wait("tensor", cDV, cDV.n)
            else:
                qs.wait("tensor", iq)
                if cch == 0:
                    K.wait("tensor", cDV, sT_ready)
            vs.wait("tensor", iv)
            K.wait("tensor", cAC, cAC.n)
            ol = ol_t[sl]
            oss.wait_free("vector")
            for dv in range(8):
                bk = dv % 4
                mms = []
                for dt in range(ND):
                    mms.append(lambda e, bk=bk, dt=dt, dv=dv, sl=sl:
                               e.matmul(
                                   PB[bk][:, 0:ACH],
                                   R2[dt][:, dv * 128:(dv + 1) * 128],
                                   qc_t[sl][dt][:, 0:ACH],
                                   start=(dt == 0), stop=False))
                for jt in range(2):
                    mms.append(lambda e, bk=bk, jt=jt, dv=dv, cch=cch:
                               e.matmul(
                                   PB[bk][:, 0:ACH],
                                   vt_t[jt][:, dv * 128:(dv + 1) * 128],
                                   sT[cch][jt][:, 0:ACH],
                                   start=False, stop=(jt == 1)))
                evs = [(lambda e, bk=bk, ol=ol, dv=dv: e.tensor_copy(
                    out=ol[dv][:, :], in_=PB[bk][:, 0:ACH]))]
                mm_group(mms, evs, bk)
            qs.set_done(iq, cPE)
            oss.store(lambda e, w=ol_W[sl], cch=cch, sp=sp: e.dma_start(
                out=o_loc.rearrange("(dv p) t -> p dv t", p=128)
                [:, sp * 8:(sp + 1) * 8, cch * ACH:(cch + 1) * ACH],
                in_=w), cDV)
            kts.wait("tensor", ikt)
            for dt in range(ND):
                reg = dt % 2
                keys = ["pw"] if reg == 0 else [4, 5]
                for kk in keys:
                    K.bank_wait(kk)
                for jt in range(2):
                    for nh in range(2):
                        lastm = (jt == 1 and nh == 1)
                        if reg == 0:
                            K.emit("tensor", lambda e, jt=jt, nh=nh, dt=dt:
                                   e.matmul(
                                       PW[:, nh * 512:(nh + 1) * 512],
                                       kt_t[jt][:, dt * 128:(dt + 1) * 128],
                                       vt_t[jt][:, nh * 512:(nh + 1) * 512],
                                       start=(jt == 0), stop=(jt == 1)),
                                   cPE if lastm else None, 1)
                        else:
                            K.emit("tensor", lambda e, jt=jt, nh=nh, dt=dt:
                                   e.matmul(
                                       PB[4 + nh][:, 0:512],
                                       kt_t[jt][:, dt * 128:(dt + 1) * 128],
                                       vt_t[jt][:, nh * 512:(nh + 1) * 512],
                                       start=(jt == 0), stop=(jt == 1)),
                                   cPE if lastm else None, 1)
                K.wait("vector", cPE, cPE.n)
                K.wait("vector", cAC, kvev.get(dt, 0))
                if reg == 0:
                    K.emit("vector", lambda e, dt=dt: e.tensor_tensor(
                        out=R1[dt][:, :], in0=PW[:, 0:1024],
                        in1=R1[dt][:, :], op=ALU.add), cDV, 1)
                else:
                    K.emit("vector", lambda e, dt=dt: e.tensor_tensor(
                        out=R1[dt][:, 0:512], in0=PB[4][:, 0:512],
                        in1=R1[dt][:, 0:512], op=ALU.add))
                    K.emit("vector", lambda e, dt=dt: e.tensor_tensor(
                        out=R1[dt][:, 512:1024], in0=PB[5][:, 0:512],
                        in1=R1[dt][:, 512:1024], op=ALU.add), cDV, 1)
                for kk in keys:
                    K.bank_set(kk, cDV)
                if cch < NACH - 1:
                    K.wait("scalar", cDV, cDV.n)
                    K.emit("scalar", lambda e, dt=dt: e.copy(
                        out=R2[dt][:, 0:1024], in_=R1[dt][:, :]), cAC, 1)
                    kvev[dt] = cAC.n
            kts.set_done(ikt, cPE)
            vs.set_done(iv, cPE)
        if sp == 0:
            sT_ready = cDV.n
        for dt in range(ND):
            sndst.wait_free("vector")
            sslot = snd[dt % 2]
            K.emit("vector", lambda e, dt=dt, s=sslot: e.tensor_single_scalar(
                out=s[:, :], in_=R1[dt][:, :], scalar=smask[:, 0:1],
                op=ALU.mult), cDV, 1)
            sndst.store(lambda e, dt=dt, sp=sp, s=sslot: e.dma_start(
                out=cc_kv_in[sp][dt * 128:(dt + 1) * 128, :],
                in_=s[:, :]), cDV)
        for ch in sndst.sch:
            K.wait("gpsimd", ch, ch.n)
        K.emit("gpsimd", lambda e, sp=sp: e.collective_compute(
            "AllReduce", ALU.add,
            replica_groups=[[0, 1], [2, 3], [4, 5], [6, 7]],
            ins=[cc_kv_in[sp]], outs=[cc_kv_out[sp]]), cCC, 1)
    R3.reset(m3p3)
    if upto == "scan_a":
        return K

    K.fence((cPE, cDV, cAC, cGP), "p6a")
    m3p6 = R3.mark()
    carr_t, carr_W = [], []
    for i in range(2):
        t_, w_ = R3.carve_block(128, BF16, ND)
        carr_t.append(t_)
        carr_W.append(w_)
    olc = R3.carve(2048, BF16, 2)
    gc = R3.carve(2048, BF16, 2)
    ogst = R3.carve(2048, BF16, 2)
    cq6 = K.chan("cq6")
    K.emit("sync", lambda e: e.dma_start(out=R2W, in_=fmv(q_fm)), cq6, 16)
    cs6 = Stream(K, "p6c", 2)
    ls6 = Stream(K, "p6l", 2)
    gs6 = Stream(K, "p6g", 2)
    ost6 = OStream(K, "p6o", 2)
    K.wait("tensor", cq6, cq6.n)
    for dv in range(ND):
        sl = dv % 2
        if dv == 0:
            K.wait("sync", cCC, cCC.n - 1)
        elif dv == 8:
            K.wait("sync", cCC, cCC.n)
        ic = cs6.issue(lambda e, sl=sl, dv=dv: e.dma_start(
            out=carr_W[sl],
            in_=fmv(cc_kv_out[dv // 8])[:, :, (dv % 8) * 128:
                                        (dv % 8 + 1) * 128]))
        il = ls6.issue(lambda e, sl=sl, dv=dv: e.dma_start(
            out=olc[sl][:, :], in_=o_loc[dv * 128:(dv + 1) * 128, :]))
        ig = gs6.issue(lambda e, sl=sl, dv=dv: e.dma_start(
            out=gc[sl][:, :], in_=g_fm[dv * 128:(dv + 1) * 128, :]))
        cs6.wait("tensor", ic)
        ls6.wait("vector", il)
        gs6.wait("vector", ig)
        st = ogst[dv % 2]
        ost6.wait_free("vector")
        for c in range(4):
            bk = next_pb()
            mms = [(lambda e, bk=bk, sl=sl, dt=dt, c=c: e.matmul(
                PB[bk][:, 0:512], carr_t[sl][dt],
                R2[dt][:, c * 512:(c + 1) * 512],
                start=(dt == 0), stop=(dt == ND - 1)))
                for dt in range(ND)]
            evs = [
                (lambda e, bk=bk, sl=sl, c=c, st=st: e.scalar_tensor_tensor(
                    out=st[:, c * 512:(c + 1) * 512], in0=PB[bk][:, 0:512],
                    scalar=umask[:, 0:1],
                    in1=olc[sl][:, c * 512:(c + 1) * 512],
                    op0=ALU.mult, op1=ALU.add)),
                (lambda e, sl=sl, c=c, st=st: e.tensor_tensor(
                    out=st[:, c * 512:(c + 1) * 512],
                    in0=st[:, c * 512:(c + 1) * 512],
                    in1=gc[sl][:, c * 512:(c + 1) * 512], op=ALU.mult)),
            ]
            mm_group(mms, evs, bk)
        cs6.set_done(ic, cPE)
        ls6.set_done(il, cDV)
        gs6.set_done(ig, cDV)
        ost6.store(lambda e, st=st, dv=dv: e.dma_start(
            out=og_hbm[dv * 128:(dv + 1) * 128, :], in_=st[:, :]), cDV)

    K.fence((cPE, cDV, cAC, cGP), "p6b")
    R3.reset(m3p6)
    cog6 = K.chan("cog6")
    K.emit("sync", lambda e: e.dma_start(out=R2W, in_=fmv(og_hbm)), cog6, 16)
    wp6 = Stream(K, "p6w", 2)
    x06 = Stream(K, "p6x", 2)
    xst6 = OStream(K, "p6y", 2)
    wpbt, wpbW = [], []
    wpft, wpfW = [], []
    for i in range(2):
        t_, w_ = R3.carve_block(128, BF16, ND)
        wpbt.append(t_)
        wpbW.append(w_)
        t_, w_ = R3.carve_block(128, F32, ND)
        wpft.append(t_)
        wpfW.append(w_)
    x0sl = R3.carve(2048, F32, 2)
    x1st = R3.carve(2048, F32, 2)
    ccast6 = K.chan("cc6")
    K.wait("tensor", cog6, cog6.n)
    bfree6 = {}
    for ht in range(ND):
        sl = ht % 2
        iw = wp6.issue(lambda e, sl=sl, ht=ht: e.dma_start(
            out=wpfW[sl], in_=fmv(wT["p"])[:, :, ht * 128:(ht + 1) * 128]))
        ix = x06.issue(lambda e, sl=sl, ht=ht: e.dma_start(
            out=x0sl[sl][:, :], in_=fmv(x0)[:, ht, :]))
        wp6.wait("vector", iw)
        if sl in bfree6:
            K.wait("vector", cPE, bfree6[sl])
        K.emit("vector", lambda e, sl=sl: e.tensor_copy(
            out=wpbW[sl], in_=wpfW[sl]), ccast6, 1)
        wp6.set_done(iw, cDV)
        x06.wait("vector", ix)
        st = x1st[ht % 2]
        xst6.wait_free("vector")
        K.wait("tensor", ccast6, ccast6.n)
        for c in range(4):
            bk = next_pb()
            mms = [(lambda e, bk=bk, sl=sl, dvv=dvv, c=c: e.matmul(
                PB[bk][:, 0:512], wpbt[sl][dvv],
                R2[dvv][:, c * 512:(c + 1) * 512],
                start=(dvv == 0), stop=(dvv == ND - 1)))
                for dvv in range(ND)]
            evs = [(lambda e, bk=bk, sl=sl, c=c, st=st: e.tensor_tensor(
                out=st[:, c * 512:(c + 1) * 512], in0=PB[bk][:, 0:512],
                in1=x0sl[sl][:, c * 512:(c + 1) * 512], op=ALU.add))]
            mm_group(mms, evs, bk)
        bfree6[sl] = cPE.n
        x06.set_done(ix, cDV)
        xst6.store(lambda e, st=st, ht=ht: e.dma_start(
            out=xres1[ht * 128:(ht + 1) * 128, :], in_=st[:, :]), cDV)
    R3.reset(m3p6)
    if upto == "x1":
        return K

    ICH = [(0, 256), (256, 256), (512, 256), (768, 256), (1024, 256),
           (1280, 128)]

    def ffn(src_dram, dst_dram, tag):
        K.fence((cPE, cDV, cAC, cGP), tag)
        m3ffn = R3.mark()
        cwrows = R3.carve(PAN, F32)
        for pan in range(2):
            t0 = pan * PAN
            gate_sc, m3f = ln_pass(
                src_dram, t0, PAN,
                [R2[dt][:, 0:PAN] for dt in range(ND)], False,
                keep_x=R1, out_guard=(cPE, cPE.n), gate=True)
            gt = R3.carve(8, F32, 4)
            for ts_ in range(8):
                s = gate_sc[:, ts_ * 8:(ts_ + 1) * 8]
                K.wait("scalar", cDV, cDV.n)
                K.emit("scalar", lambda e, s=s: e.activation(
                    out=gt[0][:, 0:8], in_=s, func=AF.Abs), cAC, 1)
                K.emit("scalar", lambda e: e.activation(
                    out=gt[1][:, 0:8], in_=gt[0][:, 0:8], func=AF.Exp,
                    scale=-1.0), cAC, 1)
                K.emit("scalar", lambda e: e.activation(
                    out=gt[1][:, 0:8], in_=gt[1][:, 0:8], func=AF.Ln,
                    bias=fc[1.0][:, 0:1]), cAC, 1)
                K.emit("scalar", lambda e, s=s: e.activation(
                    out=gt[2][:, 0:8], in_=s, func=AF.Relu), cAC, 1)
                K.wait("vector", cAC, cAC.n)
                K.emit("vector", lambda e: e.tensor_tensor(
                    out=gt[0][:, 0:8], in0=gt[1][:, 0:8], in1=gt[2][:, 0:8],
                    op=ALU.add), cDV, 1)
                K.wait("scalar", cDV, cDV.n)
                K.emit("scalar", lambda e: e.activation(
                    out=gt[1][:, 0:8], in_=gt[0][:, 0:8], func=AF.Ln),
                    cAC, 1)
                K.emit("scalar", lambda e: e.activation(
                    out=gt[1][:, 0:8], in_=gt[1][:, 0:8], func=AF.Exp,
                    scale=0.5), cAC, 1)
                K.wait("vector", cAC, cAC.n)
                emv(lambda e: e.tensor_tensor(
                    out=gt[2][:, 0:8], in0=gt[1][:, 0:8], in1=gbb[:, :],
                    op=ALU.add))
                emv(lambda e: e.reduce_max(
                    mcol[:, 0:1], gt[2][:, 0:8], axis=AXX))
                emv(lambda e: e.tensor_single_scalar(
                    out=gt[3][:, 0:8], in_=gt[2][:, 0:8],
                    scalar=mcol[:, 0:1], op=ALU.is_ge))
                emv(lambda e: e.scalar_tensor_tensor(
                    out=gt[3][:, 0:8], in0=gt[3][:, 0:8],
                    scalar=fc[-1e30][:, 0:1],
                    in1=gt[2][:, 0:8], op0=ALU.mult, op1=ALU.add))
                emv(lambda e: e.reduce_max(
                    m2col[:, 0:1], gt[3][:, 0:8], axis=AXX))
                emv(lambda e: e.tensor_single_scalar(
                    out=gt[3][:, 0:8], in_=gt[2][:, 0:8],
                    scalar=m2col[:, 0:1], op=ALU.is_ge))
                emv(lambda e: e.tensor_tensor(
                    out=gt[0][:, 0:8], in0=gt[1][:, 0:8], in1=gt[3][:, 0:8],
                    op=ALU.mult))
                bk = next_pb()
                K.wait("tensor", cDV, cDV.n)
                mm_group(
                    [lambda e, bk=bk: e.transpose(
                        PB[bk][0:8, 0:128], gt[0][:, 0:8], ident[:, :])],
                    [lambda e, bk=bk, ts_=ts_: e.tensor_copy(
                        out=cwrows[0:8, ts_ * 128:(ts_ + 1) * 128],
                        in_=PB[bk][0:8, 0:128])], bk)
            if tag == "f1":
                cdbg = K.chan("cdbg")
                K.wait("sync", cDV, cDV.n)
                K.emit("sync", lambda e, pan=pan: e.dma_start(
                    out=dbg_cw[0:8, pan * PAN:(pan + 1) * PAN],
                    in_=cwrows[0:8, :]), cdbg, 16)
                K.emit("sync", lambda e, pan=pan: e.dma_start(
                    out=dbg_gs[:, pan * 64:(pan + 1) * 64],
                    in_=gate_sc[:, :]), cdbg, 16)
            R3.reset(m3f)
            w13f, w13fW = [], []
            for i in range(2):
                t1_, w1_ = R3.carve_block(256, BF16, ND)
                t3_, w3_ = R3.carve_block(256, BF16, ND)
                w13f.append((t1_, t3_))
                w13fW.append((w1_, w3_))
            w2sl, w2slW = [], []
            for i in range(2):
                t_, w_ = R3.carve_block(128, BF16, 11)
                w2sl.append(t_)
                w2slW.append(w_)
            cwb = R3.carve(PAN, F32)
            s1sl = R3.carve(512, F32, 2)
            hcsl = R3.carve(512, F32, 2)
            w13s = Stream(K, "fw13", 2)
            w2s = Stream(K, "fw2", 2)
            K.wait("sync", cPE, cPE.n)
            K.wait("sync", cDV, cDV.n)
            for e_ in range(NEXP):
                if e_ < 8:
                    K.wait("tensor", cDV, cDV.n)
                    for half in range(2):
                        bk = next_pb()
                        mm_group(
                            [lambda e, bk=bk, e_=e_, half=half: e.matmul(
                                PB[bk][:, 0:512],
                                sel[0:8, e_ * 128:(e_ + 1) * 128],
                                cwrows[0:8, half * 512:(half + 1) * 512],
                                start=True, stop=True)],
                            [lambda e, bk=bk, half=half: e.tensor_copy(
                                out=cwb[:, half * 512:(half + 1) * 512],
                                in_=PB[bk][:, 0:512])], bk)
                for ici, (i0, iw) in enumerate(ICH):
                    sl = ici % 2
                    it13 = w13s.issue([
                        (lambda e, sl=sl, e_=e_, i0=i0, iw=iw: e.dma_start(
                            out=w13fW[sl][0][:, :, 0:iw],
                            in_=fmv(w1T_all[e_])[:, :, i0:i0 + iw])),
                        (lambda e, sl=sl, e_=e_, i0=i0, iw=iw: e.dma_start(
                            out=w13fW[sl][1][:, :, 0:iw],
                            in_=fmv(w3T_all[e_])[:, :, i0:i0 + iw]))])
                    w13s.wait("tensor", it13)
                    for it in range(iw // 128):
                        itg = i0 // 128 + it
                        for tc in range(2):
                            bk1 = next_pb()
                            mms1 = [(lambda e, bk1=bk1, sl=sl, dt=dt, it=it,
                                     tc=tc: e.matmul(
                                         PB[bk1][:, 0:512],
                                         w13f[sl][0][dt]
                                         [:, it * 128:(it + 1) * 128],
                                         R2[dt][:, tc * 512:(tc + 1) * 512],
                                         start=(dt == 0),
                                         stop=(dt == ND - 1)))
                                    for dt in range(ND)]
                            ss = s1sl[tc]
                            evs1 = [(lambda e, bk1=bk1, ss=ss: e.activation(
                                out=ss[:, 0:512], in_=PB[bk1][:, 0:512],
                                func=AF.Silu))]
                            mm_group(mms1, evs1, bk1, ev="scalar")
                            bk3 = next_pb()
                            mms3 = [(lambda e, bk3=bk3, sl=sl, dt=dt, it=it,
                                     tc=tc: e.matmul(
                                         PB[bk3][:, 0:512],
                                         w13f[sl][1][dt]
                                         [:, it * 128:(it + 1) * 128],
                                         R2[dt][:, tc * 512:(tc + 1) * 512],
                                         start=(dt == 0),
                                         stop=(dt == ND - 1)))
                                    for dt in range(ND)]
                            hdst = R2[itg][:, 1024 + tc * 512:
                                           1024 + (tc + 1) * 512]
                            hc = hcsl[tc]
                            K.wait("vector", cAC, cAC.n)
                            if e_ < 8:
                                evs3 = [
                                    (lambda e, bk3=bk3, hc=hc, tc=tc:
                                     e.tensor_tensor(
                                         out=hc[:, 0:512],
                                         in0=PB[bk3][:, 0:512],
                                         in1=cwb[:, tc * 512:(tc + 1) * 512],
                                         op=ALU.mult)),
                                    (lambda e, hc=hc, ss=ss, hdst=hdst:
                                     e.tensor_tensor(
                                         out=hdst, in0=ss[:, 0:512],
                                         in1=hc[:, 0:512], op=ALU.mult)),
                                ]
                            else:
                                evs3 = [(lambda e, bk3=bk3, ss=ss, hdst=hdst:
                                         e.tensor_tensor(
                                             out=hdst, in0=ss[:, 0:512],
                                             in1=PB[bk3][:, 0:512],
                                             op=ALU.mult))]
                            mm_group(mms3, evs3, bk3)
                    w13s.set_done(it13, cPE)
                K.wait("tensor", cDV, cDV.n)
                for dout in range(ND):
                    sl = dout % 2
                    iw2 = w2s.issue(lambda e, sl=sl, e_=e_, dout=dout:
                                    e.dma_start(
                                        out=w2slW[sl],
                                        in_=w2T_all[e_].rearrange(
                                            "(it p) d -> p it d", p=128)
                                        [:, :, dout * 128:(dout + 1) * 128]))
                    w2s.wait("tensor", iw2)
                    for tc in range(2):
                        bk = next_pb()
                        mms = [(lambda e, bk=bk, sl=sl, it=it, tc=tc:
                                e.matmul(
                                    PB[bk][:, 0:512], w2sl[sl][it],
                                    R2[it][:, 1024 + tc * 512:
                                           1024 + (tc + 1) * 512],
                                    start=(it == 0), stop=(it == 10)))
                               for it in range(11)]
                        evs = [(lambda e, bk=bk, dout=dout, tc=tc:
                                e.tensor_tensor(
                                    out=R1[dout][:, tc * 512:(tc + 1) * 512],
                                    in0=PB[bk][:, 0:512],
                                    in1=R1[dout][:, tc * 512:(tc + 1) * 512],
                                    op=ALU.add))]
                        mm_group(mms, evs, bk)
                    w2s.set_done(iw2, cPE)
            cyst = K.chan(tag + "y")
            K.wait("sync", cDV, cDV.n)
            K.emit("sync", lambda e, t0=t0: e.dma_start(
                out=fmv(dst_dram)[:, :, t0:t0 + PAN], in_=R1W), cyst, 16)
            K.note_pending(cyst)
            K.wait("sync", cyst, cyst.n)
            R3.reset(m3f)
        R3.reset(m3ffn)

    ffn(xres1, xres2, "f1")
    if upto == "x2":
        return K

    K.fence((cPE, cDV, cAC, cGP), "p8")
    m38 = R3.mark()
    xcols = R3.carve(64, F32)
    hl_t = R3.carve(16, F32)
    rb8 = R3.carve(4, F32)
    c8 = K.chan("c8")
    xcv = xcols[:, :].rearrange("p (dt c) -> p dt c", dt=16)
    K.emit("sync", lambda e: e.dma_start(
        out=xcv, in_=fmv(xres2)[:, :, T - 4:T]), c8, 16)
    K.wait("scalar", c8)
    K.emit("scalar", lambda e: e.activation(
        out=hl_t[:, :], in_=xcv[:, :, 3].opt(), func=AF.Square), cAC, 1)
    K.wait("tensor", c8)
    K.wait("tensor", cAC, cAC.n)
    bkA = next_pb()
    bkB = next_pb()
    K.bank_wait(bkA)
    K.bank_wait(bkB)
    for dt in range(ND):
        K.emit("tensor", lambda e, dt=dt, bkA=bkA: e.matmul(
            PB[bkA][0:1, 0:1], onec[:, 0:1], xcv[:, dt, 3:4].opt(),
            start=(dt == 0), stop=(dt == ND - 1)))
        K.emit("tensor", lambda e, dt=dt, bkB=bkB: e.matmul(
            PB[bkB][0:1, 0:1], onec[:, 0:1], hl_t[:, dt:dt + 1],
            start=(dt == 0), stop=(dt == ND - 1)),
            cPE if dt == ND - 1 else None, 1)
    K.wait("vector", cPE, cPE.n)
    emv(lambda e, bkA=bkA: e.tensor_single_scalar(
        out=rb8[0:1, 0:1], in_=PB[bkA][0:1, 0:1], scalar=1.0 / D,
        op=ALU.mult))
    emv(lambda e, bkB=bkB: e.tensor_single_scalar(
        out=rb8[0:1, 1:2], in_=PB[bkB][0:1, 0:1], scalar=1.0 / D,
        op=ALU.mult))
    K.bank_set(bkA, cDV)
    K.bank_set(bkB, cDV)
    emv(lambda e: e.tensor_tensor(
        out=rb8[0:1, 2:3], in0=rb8[0:1, 0:1], in1=rb8[0:1, 0:1],
        op=ALU.mult))
    emv(lambda e: e.tensor_tensor(
        out=rb8[0:1, 2:3], in0=rb8[0:1, 1:2],
        in1=rb8[0:1, 2:3], op=ALU.subtract))
    K.wait("scalar", cDV)
    K.emit("scalar", lambda e: e.activation(
        out=rb8[0:1, 3:4], in_=rb8[0:1, 2:3], func=AF.Ln,
        bias=fc[EPS_LN][0:1, 0:1]), cAC, 1)
    K.emit("scalar", lambda e: e.activation(
        out=rb8[0:1, 3:4], in_=rb8[0:1, 3:4], func=AF.Exp,
        scale=-0.5), cAC, 1)
    K.wait("tensor", cAC, cAC.n)
    K.wait("tensor", cDV, cDV.n)
    bkA = next_pb()
    mm_group([lambda e, bkA=bkA: e.matmul(
        PB[bkA][:, 0:1], oner[0:1, :], rb8[0:1, 0:1], start=True,
        stop=True)],
        [lambda e, bkA=bkA: e.tensor_copy(out=mcol[:, 0:1],
                                          in_=PB[bkA][:, 0:1])], bkA)
    bkB = next_pb()
    mm_group([lambda e, bkB=bkB: e.matmul(
        PB[bkB][:, 0:1], oner[0:1, :], rb8[0:1, 3:4], start=True,
        stop=True)],
        [lambda e, bkB=bkB: e.tensor_copy(out=rscol[:, 0:1],
                                          in_=PB[bkB][:, 0:1])], bkB)
    K.wait("vector", cDV, cDV.n)
    for dt in range(ND):
        emv(lambda e, dt=dt: e.tensor_tensor(
            out=hl_t[:, dt:dt + 1], in0=xcv[:, dt, 3:4].opt(),
            in1=mcol[:, 0:1], op=ALU.subtract))
        emv(lambda e, dt=dt: e.tensor_single_scalar(
            out=hl_t[:, dt:dt + 1], in_=hl_t[:, dt:dt + 1],
            scalar=rscol[:, 0:1], op=ALU.mult))
        emv(lambda e, dt=dt: e.tensor_single_scalar(
            out=hl_t[:, dt:dt + 1], in_=hl_t[:, dt:dt + 1],
            scalar=smask[:, 0:1], op=ALU.mult))
    ch8 = K.chan("ch8")
    K.wait("sync", cDV, cDV.n)
    K.emit("sync", lambda e: e.dma_start(out=cc_h_in, in_=hl_t[:, :]),
           ch8, 16)
    K.wait("gpsimd", ch8, ch8.n)
    K.emit("gpsimd", lambda e: e.collective_compute(
        "AllReduce", ALU.add, replica_groups=[[0, 1], [2, 3], [4, 5], [6, 7]],
        ins=[cc_h_in], outs=[cc_h_out]), cCC, 1)
    K.wait("sync", cCC, cCC.n)
    K.emit("sync", lambda e: e.dma_start(out=hprev_u[:, :], in_=cc_h_out),
           ch8, 16)
    K.wait("vector", ch8, ch8.n)
    K.emit("vector", lambda e: e.tensor_single_scalar(
        out=hprev_u[:, :], in_=hprev_u[:, :], scalar=umask[:, 0:1],
        op=ALU.mult), cDV, 1)
    R3.reset(m38)

    ws8 = Stream(K, "p8w", 1)
    os8 = OStream(K, "p8r", 2)
    hcol_src = {0: hprev_u, 1: hkeep}
    wbb_guard = {"v": 0}
    for pan in range(2):
        t0p = pan * PAN
        hpt = [R2[dt][:, 0:PAN] for dt in range(ND)]
        xmt = [R2[dt][:, PAN:2 * PAN] for dt in range(ND)]
        ln_pass(xres2, t0p, PAN, hpt, False, out_guard=(cPE, cPE.n))
        m3pan = R3.mark()
        kp_t, kp_W = R3.carve_block(PAN, BF16, ND)
        wbf8t, wbf8W = R3.carve_block(128, F32, ND)
        wbb8t, wbb8W = R3.carve_block(128, BF16, ND)
        mt8 = R3.carve(PAN, F32)
        vtmp8 = R3.carve(512, F32)
        rst8 = R3.carve(PAN, BF16, 2)
        if pan == 0:
            K.wait("vector", cDV, cDV.n)
            for dt in range(ND):
                K.emit("vector", lambda e, dt=dt, hpt=hpt: e.tensor_copy(
                    out=hkeep[:, dt:dt + 1],
                    in_=hpt[dt][:, PAN - 1:PAN]), cDV, 1)
        hsrc = hcol_src[pan]
        kb_done = {}
        for wnm, mnm in (("r", "r"), ("kb", "k"), ("vb", "v")):
            if wnm == "vb":
                K.wait("vector", cAC, cAC.n)
            K.wait("vector", cPE, cPE.n)
            for dt in range(ND):
                emv(lambda e, dt=dt, mnm=mnm, hpt=hpt:
                    e.tensor_single_scalar(
                        out=mt8[:, :], in_=hpt[dt][:, :],
                        scalar=mixs[mnm][:, dt:dt + 1], op=ALU.mult))
                emv(lambda e, dt=dt, mnm=mnm, hpt=hpt, xmt=xmt:
                    e.scalar_tensor_tensor(
                        out=xmt[dt][:, 1:PAN], in0=hpt[dt][:, 0:PAN - 1],
                        scalar=imixs[mnm][:, dt:dt + 1],
                        in1=mt8[:, 1:PAN], op0=ALU.mult, op1=ALU.add))
                emv(lambda e, dt=dt, mnm=mnm, hsrc=hsrc,
                    xmt=xmt: e.scalar_tensor_tensor(
                        out=xmt[dt][:, 0:1], in0=hsrc[:, dt:dt + 1],
                        scalar=imixs[mnm][:, dt:dt + 1],
                        in1=mt8[:, 0:1], op0=ALU.mult, op1=ALU.add))
            K.wait("tensor", cDV, cDV.n)
            K.wait("sync", cPE, cPE.n)
            K.wait("sync", cDV, cDV.n)
            for ot in range(ND):
                iw = ws8.issue(lambda e, wnm=wnm, ot=ot: e.dma_start(
                    out=wbf8W, in_=fmv(wT[wnm])[:, :,
                                                ot * 128:(ot + 1) * 128]))
                ws8.wait("vector", iw)
                K.wait("vector", cPE, wbb_guard["v"])
                K.emit("vector", lambda e: e.tensor_copy(
                    out=wbb8W, in_=wbf8W), cDV, 1)
                ws8.set_done(iw, cDV)
                K.wait("tensor", cDV, cDV.n)
                for tc in range(2):
                    bk = next_pb()
                    mms = [(lambda e, bk=bk, dt=dt, tc=tc, xmt=xmt: e.matmul(
                        PB[bk][:, 0:512], wbb8t[dt],
                        xmt[dt][:, tc * 512:(tc + 1) * 512],
                        start=(dt == 0), stop=(dt == ND - 1)))
                        for dt in range(ND)]
                    if wnm == "r":
                        st = rst8[ot % 2]
                        if tc == 0:
                            os8.wait_free("scalar")
                        evs = [(lambda e, bk=bk, st=st, tc=tc: e.activation(
                            out=st[:, tc * 512:(tc + 1) * 512],
                            in_=PB[bk][:, 0:512], func=AF.Sigmoid))]
                        mm_group(mms, evs, bk, ev="scalar")
                    elif wnm == "kb":
                        evs = [(lambda e, bk=bk, ot=ot, tc=tc, kp_t=kp_t:
                                e.activation(
                                    out=kp_t[ot][:, tc * 512:(tc + 1) * 512],
                                    in_=PB[bk][:, 0:512], func=AF.Copy))]
                        mm_group(mms, evs, bk, ev="scalar")
                    else:
                        evs = [
                            (lambda e, bk=bk: e.tensor_copy(
                                out=vtmp8[:, 0:512], in_=PB[bk][:, 0:512])),
                            (lambda e, ot=ot, tc=tc, t0p=t0p,
                             kp_t=kp_t: e.tensor_tensor(
                                 out=R1bf[ot][:, t0p + tc * 512:
                                              t0p + (tc + 1) * 512],
                                 in0=kp_t[ot][:, tc * 512:(tc + 1) * 512],
                                 in1=vtmp8[:, 0:512], op=ALU.mult)),
                        ]
                        mm_group(mms, evs, bk)
                wbb_guard["v"] = cPE.n
                if wnm == "r":
                    os8.store(lambda e, ot=ot, t0p=t0p, st=rst8[ot % 2]:
                              e.dma_start(
                                  out=r_hbm[ot * 128:(ot + 1) * 128,
                                            t0p:t0p + PAN],
                                  in_=st[:, :]), cAC)
        R3.reset(m3pan)

    if upto == "b_proj":
        cdk = K.chan("cdk0")
        K.wait("sync", cDV, cDV.n)
        K.wait("sync", cPE, cPE.n)
        K.emit("sync", lambda e: e.dma_start(
            out=fmv(dbg_kv), in_=R1W.bitcast(BF16)), cdk, 16)
        return K
    K.fence((cPE, cDV, cAC, cGP), "p8s")
    m3s = R3.mark()
    wbt8 = R3.carve(T, F32)
    sc8 = R3.carve(T, F32)
    t28 = R3.carve(T, F32)
    kvf8 = R3.carve(T, F32, 2)
    rsl8 = R3.carve(T, BF16, 2)
    ones8 = R3.carve(T, F32)
    emv(lambda e: e.memset(ones8[:, :], 1.0))
    kvf_free = {}
    for dt in range(ND):
        kf = kvf8[dt % 2]
        K.wait("scalar", cDV, kvf_free.get(dt % 2, 0))
        K.emit("scalar", lambda e, dt=dt, kf=kf: e.copy(
            out=kf[:, :], in_=R1bf[dt][:, :]), cAC, 1)
        emv(lambda e, dt=dt: e.tensor_single_scalar(
            out=wbt8[:, :], in_=ones8[:, :], scalar=w_cols[:, dt:dt + 1],
            op=ALU.mult))
        K.wait("vector", cAC, cAC.n)
        emv(lambda e, dt=dt, kf=kf: e.tensor_tensor_scan(
            sc8[:, :], wbt8[:, :], kf[:, :], 0.0, ALU.mult, ALU.add))
        emv(lambda e, dt=dt: e.tensor_single_scalar(
            out=s_send[:, dt:dt + 1], in_=sc8[:, T - 1:T],
            scalar=smask[:, 0:1], op=ALU.mult))
        kvf_free[dt % 2] = cDV.n
    cs8 = K.chan("cs8")
    K.wait("sync", cDV, cDV.n)
    K.emit("sync", lambda e: e.dma_start(out=cc_s_in, in_=s_send[:, :]),
           cs8, 16)
    K.wait("gpsimd", cs8, cs8.n)
    K.emit("gpsimd", lambda e: e.collective_compute(
        "AllReduce", ALU.add, replica_groups=[[0, 1], [2, 3], [4, 5], [6, 7]],
        ins=[cc_s_in], outs=[cc_s_out]), cCC, 1)
    K.wait("sync", cCC, cCC.n)
    K.emit("sync", lambda e: e.dma_start(out=s_use[:, :], in_=cc_s_out),
           cs8, 16)
    K.wait("vector", cs8, cs8.n)
    K.emit("vector", lambda e: e.tensor_single_scalar(
        out=s_use[:, :], in_=s_use[:, :], scalar=umask[:, 0:1],
        op=ALU.mult), cDV, 1)

    if "dbg_kv" in dump:
        cdk = K.chan("cdk")
        K.wait("sync", cDV, cDV.n)
        K.emit("sync", lambda e: e.dma_start(
            out=fmv(dbg_kv), in_=R1W.bitcast(BF16)), cdk, 16)
    if upto == "b_scan1":
        return K
    rs8 = Stream(K, "p8rr", 2)
    kvf_free = {}
    for dt in range(ND):
        ir = rs8.issue(lambda e, dt=dt: e.dma_start(
            out=rsl8[dt % 2][:, :], in_=r_hbm[dt * 128:(dt + 1) * 128, :]))
        kf = kvf8[dt % 2]
        K.wait("scalar", cDV, kvf_free.get(dt % 2, 0))
        K.emit("scalar", lambda e, dt=dt, kf=kf: e.copy(
            out=kf[:, :], in_=R1bf[dt][:, :]), cAC, 1)
        emv(lambda e, dt=dt: e.tensor_single_scalar(
            out=wbt8[:, :], in_=ones8[:, :], scalar=w_cols[:, dt:dt + 1],
            op=ALU.mult))
        K.wait("vector", cAC, cAC.n)
        emv(lambda e, dt=dt, kf=kf: e.tensor_tensor_scan(
            sc8[:, :], wbt8[:, :], kf[:, :], s_use[:, dt:dt + 1],
            ALU.mult, ALU.add))
        emv(lambda e, dt=dt, kf=kf: e.scalar_tensor_tensor(
            out=t28[:, 1:T], in0=kf[:, 1:T],
            scalar=u_cols[:, dt:dt + 1], in1=sc8[:, 0:T - 1],
            op0=ALU.mult, op1=ALU.add))
        emv(lambda e, dt=dt, kf=kf: e.scalar_tensor_tensor(
            out=t28[:, 0:1], in0=kf[:, 0:1],
            scalar=u_cols[:, dt:dt + 1], in1=s_use[:, dt:dt + 1],
            op0=ALU.mult, op1=ALU.add))
        rs8.wait("vector", ir)
        emv(lambda e, dt=dt: e.tensor_tensor(
            out=R1bf[dt][:, :], in0=t28[:, :], in1=rsl8[dt % 2][:, :],
            op=ALU.mult))
        rs8.set_done(ir, cDV)
        kvf_free[dt % 2] = cDV.n
    if "dbg_wkv" in dump:
        cdw = K.chan("cdw")
        K.wait("sync", cDV, cDV.n)
        K.emit("sync", lambda e: e.dma_start(
            out=fmv(dbg_wkv), in_=R1W.bitcast(BF16)), cdw, 16)
    R3.reset(m3s)

    K.fence((cPE, cDV, cAC, cGP), "p8o")
    m3o = R3.mark()
    wo8 = Stream(K, "p8wo", 2)
    x28 = Stream(K, "p8x2", 2)
    xo8 = OStream(K, "p8x3", 2)
    wof_t, wof_W = [], []
    wob_t, wob_W = [], []
    for i in range(2):
        t_, w_ = R3.carve_block(128, F32, ND)
        wof_t.append(t_)
        wof_W.append(w_)
        t_, w_ = R3.carve_block(128, BF16, ND)
        wob_t.append(t_)
        wob_W.append(w_)
    x2sl = R3.carve(T, F32, 2)
    x3st = R3.carve(T, F32, 2)
    cc8 = K.chan("cc8b")
    bfree8 = {}
    for ht in range(ND):
        sl = ht % 2
        iw = wo8.issue(lambda e, sl=sl, ht=ht: e.dma_start(
            out=wof_W[sl], in_=fmv(wT["o"])[:, :, ht * 128:(ht + 1) * 128]))
        ix = x28.issue(lambda e, sl=sl, ht=ht: e.dma_start(
            out=x2sl[sl][:, :], in_=fmv(xres2)[:, ht, :]))
        wo8.wait("vector", iw)
        if sl in bfree8:
            K.wait("vector", cPE, bfree8[sl])
        K.emit("vector", lambda e, sl=sl: e.tensor_copy(
            out=wob_W[sl], in_=wof_W[sl]), cc8, 1)
        wo8.set_done(iw, cDV)
        x28.wait("vector", ix)
        st = x3st[ht % 2]
        xo8.wait_free("vector")
        K.wait("tensor", cc8, cc8.n)
        if ht == 0:
            K.wait("tensor", cDV, cDV.n)
        for c in range(4):
            bk = next_pb()
            mms = [(lambda e, bk=bk, sl=sl, ot=ot, c=c: e.matmul(
                PB[bk][:, 0:512], wob_t[sl][ot],
                R1bf[ot][:, c * 512:(c + 1) * 512],
                start=(ot == 0), stop=(ot == ND - 1)))
                for ot in range(ND)]
            evs = [(lambda e, bk=bk, sl=sl, c=c, st=st: e.tensor_tensor(
                out=st[:, c * 512:(c + 1) * 512], in0=PB[bk][:, 0:512],
                in1=x2sl[sl][:, c * 512:(c + 1) * 512], op=ALU.add))]
            mm_group(mms, evs, bk)
        bfree8[sl] = cPE.n
        x28.set_done(ix, cDV)
        xo8.store(lambda e, st=st, ht=ht: e.dma_start(
            out=xres3[ht * 128:(ht + 1) * 128, :], in_=st[:, :]), cDV)
    R3.reset(m3o)
    if upto == "x3":
        return K

    ffn(xres3, out_fm, "f2")
    return K


def _shared_inputs(inputs):
    def tp(a):
        return np.ascontiguousarray(np.asarray(a).T)
    sh = {}
    sh["wqT"] = tp(inputs["L0_attn_a_Wq"])
    sh["wkT"] = tp(inputs["L0_attn_a_Wk"])
    sh["wvT"] = tp(inputs["L0_attn_a_Wv"])
    sh["wgT"] = tp(inputs["L0_attn_a_Wg"])
    sh["wpT"] = tp(inputs["L0_attn_a_Wp"])
    sh["wrT"] = tp(inputs["L0_attn_b_Wr"])
    sh["wkbT"] = tp(inputs["L0_attn_b_Wk"])
    sh["wvbT"] = tp(inputs["L0_attn_b_Wv"])
    sh["woT"] = tp(inputs["L0_attn_b_Wo"])
    sh["td_hp"] = np.ascontiguousarray(
        np.asarray(inputs["L0_attn_b_time_decay"], dtype=np.float32))
    sh["tf_hp"] = np.ascontiguousarray(
        np.asarray(inputs["L0_attn_b_time_first"], dtype=np.float32))
    sh["mixk_hp"] = np.asarray(
        inputs["L0_attn_b_time_mix_k"], dtype=np.float32).reshape(16, 128)
    sh["mixv_hp"] = np.asarray(
        inputs["L0_attn_b_time_mix_v"], dtype=np.float32).reshape(16, 128)
    sh["mixr_hp"] = np.asarray(
        inputs["L0_attn_b_time_mix_r"], dtype=np.float32).reshape(16, 128)
    sh["gwT"] = tp(inputs["L0_ffn_gate_w"])
    sh["gb_row"] = np.asarray(
        inputs["L0_ffn_gate_b"], dtype=np.float32).reshape(1, 8)
    bf = ml_dtypes.bfloat16
    w1 = np.asarray(inputs["L0_ffn_w1"])
    w3 = np.asarray(inputs["L0_ffn_w3"])
    w2 = np.asarray(inputs["L0_ffn_w2"])
    sw1 = np.asarray(inputs["L0_ffn_sw1"])
    sw3 = np.asarray(inputs["L0_ffn_sw3"])
    sw2 = np.asarray(inputs["L0_ffn_sw2"])
    sh["w1T_all"] = np.ascontiguousarray(np.concatenate(
        [w1.transpose(0, 2, 1), sw1.T[None]], axis=0)).astype(bf)
    sh["w3T_all"] = np.ascontiguousarray(np.concatenate(
        [w3.transpose(0, 2, 1), sw3.T[None]], axis=0)).astype(bf)
    sh["w2T_all"] = np.ascontiguousarray(np.concatenate(
        [w2.transpose(0, 2, 1), sw2.T[None]], axis=0)).astype(bf)
    sh["ident_f"] = np.eye(128, dtype=np.float32)
    sh["triu_m"] = np.triu(np.ones((128, 128), dtype=np.float32))
    sh["ones_col"] = np.ones((128, 1), dtype=np.float32)
    sh["ones_row"] = np.ones((1, 128), dtype=np.float32)
    selm = np.zeros((8, 8 * 128), dtype=np.float32)
    for e in range(8):
        selm[e, e * 128:(e + 1) * 128] = 1.0
    sh["sel_in"] = selm
    return sh


def make_in_maps(inputs, dump=()):
    sh = _shared_inputs(inputs)
    x = np.asarray(inputs["L0_attn_a_x"])
    in_maps = []
    for c in range(8):
        b, hf = c // 2, c % 2
        m = dict(sh)
        m["x0"] = np.ascontiguousarray(
            x[b, hf * T:(hf + 1) * T, :].T).astype(np.float32)
        m["umask_in"] = np.full((128, 1), float(hf), dtype=np.float32)
        m["smask_in"] = np.full((128, 1), float(1 - hf), dtype=np.float32)
        in_maps.append(m)
    return in_maps


def kernel(**inputs):
    K = build_nc()
    nc = K.nc
    with nc.Block() as block:
        K.finalize(block)
    in_maps = make_in_maps(inputs)
    res = run_bass_kernel_spmd(nc, in_maps, core_ids=list(range(8)))
    out = np.empty((4, 4096, D), dtype=np.float32)
    for c in range(8):
        b, hf = c // 2, c % 2
        out[b, hf * T:(hf + 1) * T, :] = res.results[c]["out_fm"].T
    return out
